# version 01
# baseline (speedup 1.0000x reference)
import os
import numpy as np
import ml_dtypes
import concourse.bass as bass
import concourse.mybir as mybir
from concourse.bass_utils import run_bass_kernel_spmd

F32 = mybir.dt.float32
BF16 = mybir.dt.bfloat16
I32 = mybir.dt.int32
AF = mybir.ActivationFunctionType
ALU = mybir.AluOpType
AX = mybir.AxisListType

SEQ = 2
T = 2048
C = 1024
NH = 16
HD = 64
WIN = 256
HID = 4096
TILE = 1024
SUB = 512
NCH = C // 128
HCH = HID // 128
NRING = 10
HG = 2
EW = 384
EPS = 1e-5
VS = NH * (HD + 1)


class Sched:
    def __init__(self):
        self.items = []
        self.count = {}
        self.waited = {}
        self.last_item = {}

    def _op(self, eng, emit, deps):
        waits = []
        for tok in deps:
            if tok is None:
                continue
            sem, val = tok
            if self.waited.get((eng, sem), 0) < val:
                waits.append((sem, val))
                self.waited[(eng, sem)] = val
        self.items.append([eng, emit, waits, []])
        self.last_item[eng] = len(self.items) - 1

    def pe(self, emit, deps=(), ms=False):
        self._op("tensor", emit, deps)
        return self.milestone("tensor") if ms else None

    def dve(self, emit, deps=(), ms=False):
        self._op("vector", emit, deps)
        return self.milestone("vector") if ms else None

    def act(self, emit, deps=(), ms=False):
        self._op("scalar", emit, deps)
        return self.milestone("scalar") if ms else None

    def pool(self, emit, deps=(), ms=False):
        self._op("gpsimd", emit, deps)
        return self.milestone("gpsimd") if ms else None

    DMA_SEMS = ["dsy", "dgp", "ws0", "ws1", "dst", "dxb", "dwa0", "dwa1",
                "dvw", "dwb0", "dwb1", "dh0", "dh1", "dh2", "dou0", "dou1"]

    def dma(self, q, emit, deps=(), sem=None):
        if sem is None:
            sem = "dsy" if q == "sync" else "dgp"
        self._op(q, emit, deps)
        self.count[sem] = self.count.get(sem, 0) + 16
        self.items[-1][3].append((sem, 16))
        return (sem, self.count[sem])

    def wait_only(self, eng, emit, deps):
        self._op(eng, emit, deps)

    def milestone(self, eng):
        name = {"tensor": "pe", "vector": "dve", "scalar": "act", "gpsimd": "pool"}[eng]
        idx = self.last_item[eng]
        self.count[name] = self.count.get(name, 0) + 1
        self.items[idx][3].append((name, 1))
        return (name, self.count[name])

    def emit_all(self, nc, sems):
        with nc.Block() as block:
            def mk(eng_key):
                def body(eng):
                    for it_eng, emit, waits, incs in self.items:
                        if it_eng != eng_key:
                            continue
                        for sem, val in waits:
                            eng.wait_ge(sems[sem], val)
                        inst = emit(eng)
                        for sem, amt in incs:
                            inst.then_inc(sems[sem], amt)
                return body
            block.tensor(mk("tensor"))
            block.vector(mk("vector"))
            block.scalar(mk("scalar"))
            block.gpsimd(mk("gpsimd"))
            block.sync(mk("sync"))


class _StopBuild(Exception):
    pass


_TRUNC = os.environ.get("KBUILD_TRUNC", "")
_DBG = bool(os.environ.get("KBUILD_DEBUG"))


def _cut(stage):
    if _TRUNC == stage:
        raise _StopBuild()


def build_program():
    nc = bass.Bass("TRN2", target_bir_lowering=False)

    x_in = nc.dram_tensor("x_in", [SEQ, C, T], F32, kind="ExternalInput").ap()
    w_attn = nc.dram_tensor("w_attn", [C, 3 * C], F32, kind="ExternalInput").ap()
    b_attn = nc.dram_tensor("b_attn", [3 * C], F32, kind="ExternalInput").ap()
    w_proj = nc.dram_tensor("w_proj", [C, C], F32, kind="ExternalInput").ap()
    b_proj = nc.dram_tensor("b_proj", [C], F32, kind="ExternalInput").ap()
    w1 = nc.dram_tensor("w1", [HID, C], F32, kind="ExternalInput").ap()
    convw = nc.dram_tensor("convw", [HID, 3], F32, kind="ExternalInput").ap()
    convb = nc.dram_tensor("convb", [HID], F32, kind="ExternalInput").ap()
    w2 = nc.dram_tensor("w2", [C, HID], F32, kind="ExternalInput").ap()
    maskc = nc.dram_tensor("maskc", [128, EW], BF16, kind="ExternalInput").ap()
    battn_pc = nc.dram_tensor("battn_pc", [128, 3 * NCH], F32, kind="ExternalInput").ap()
    bproj_pc = nc.dram_tensor("bproj_pc", [128, NCH], F32, kind="ExternalInput").ap()
    convw_pc = nc.dram_tensor("convw_pc", [128, 3 * HCH], F32, kind="ExternalInput").ap()
    convb_pc = nc.dram_tensor("convb_pc", [128, HCH], F32, kind="ExternalInput").ap()
    idc = nc.dram_tensor("idc", [128, 128], BF16, kind="ExternalInput").ap()

    out_t = nc.dram_tensor("out_t", [SEQ, C, T], F32, kind="ExternalOutput").ap()

    dbg = {}
    if _DBG:
        for nm, sh, dt in [("dbg_xn", [128, NCH * SUB], BF16), ("dbg_q", [128, NCH * SUB], BF16),
                           ("dbg_k", [128, NCH * NRING * 128], BF16), ("dbg_v", [128, NRING * VS], BF16),
                           ("dbg_o", [128, NCH * SUB], BF16), ("dbg_pj", [128, NCH * SUB], BF16),
                           ("dbg_x1", [128, NCH * SUB], BF16), ("dbg_xq", [128, NCH * SUB], BF16),
                           ("dbg_h", [128, HCH * 514], BF16), ("dbg_hq", [128, HCH * 514], BF16),
                           ("dbg_y2", [128, NCH * SUB], BF16), ("dbg_e", [128, HG * 1536], BF16)]:
            dbg[nm] = nc.dram_tensor(nm, sh, dt, kind="ExternalOutput").ap()

    w1T = nc.dram_tensor("w1T_scr", [C, HID], BF16, kind="Internal").ap()
    w2T = nc.dram_tensor("w2T_scr", [HID, C], BF16, kind="Internal").ap()
    rbounce = nc.dram_tensor("rbounce", [2 * T], F32, kind="Internal").ap()

    ctxs = []

    def S(name, shape, dt):
        g = nc.sbuf_tensor(name, shape, dt)
        t = g.__enter__()
        ctxs.append(g)
        return t

    q_t = S("q_t", [128, NCH * SUB], BF16)
    xw_t = S("xw_t", [128, NCH * SUB], BF16)
    kring = S("kring", [128, NCH * NRING * 128], BF16)
    vring = S("vring", [128, NRING * VS], BF16)
    ering = S("ering", [128, HG * 1536], BF16)
    x1ring = S("x1ring", [128, 3 * NCH * SUB], BF16)
    h2 = S("h2", [128, 2 * HCH * 514], BF16)
    wdb = S("wdb", [128, 2 * 1024], BF16)
    w2db = S("w2db", [128, 3 * 2048], BF16)
    xqt = S("xqt", [128, NCH * SUB], BF16)
    accs = S("accs", [128, SUB], BF16)
    cs1 = S("cs1", [128, SUB], BF16)
    cs2 = S("cs2", [128, SUB], BF16)
    uscr = S("uscr", [128, SUB], F32)
    iscr = S("iscr", [128, SUB], I32)
    osb = S("osb", [65, SUB], BF16)
    mask = S("mask", [128, EW], BF16)
    ident = S("ident", [128, 128], BF16)
    ones_r = S("ones_r", [1, 128], BF16)
    ones_rf = S("ones_rf", [1, 128], F32)
    ones_c = S("ones_c", [128, 1], BF16)
    ones_cf = S("ones_cf", [128, 1], F32)
    battn_c = S("battn_c", [128, 3 * NCH], F32)
    bv_r = S("bv_r", [1, C], BF16)
    bproj_c = S("bproj_c", [128, NCH], F32)
    convw_c = S("convw_c", [128, 3 * HCH], F32)
    convb_c = S("convb_c", [128, HCH], F32)
    rows = S("rows", [1, 2560], F32)
    mrbc = S("mrbc", [128, SUB], BF16)
    rrbc = S("rrbc", [128, SUB], BF16)
    hs1bc = S("hs1bc", [128, SUB], F32)
    rs1bc = S("rs1bc", [128, SUB], F32)
    cols = S("cols", [128, 32], F32)
    colsb = S("colsb", [128, 8], BF16)
    outstg = S("outstg", [128, 2 * SUB], F32)
    y2sb = q_t
    wqs = xqt
    wts = x1ring
    wraw = outstg
    sqs = cs2
    rcr = rows
    bcsb = accs
    m3bc = mrbc
    r3bc = rrbc
    m2bc = mrbc
    xb_t = h2

    def P(name, shape, dt=F32):
        g = nc.psum_tensor(name, shape, dt)
        t = g.__enter__()
        ctxs.append(g)
        return t

    ps_a = [P("ps_a0", [128, 512]), P("ps_a1", [128, 512])]
    ps_b = [P("ps_b0", [65, 512]), P("ps_b1", [65, 512])]
    ps_c = P("ps_c", [128, 512])
    ps_r1 = P("ps_r1", [1, 512])
    ps_r2 = P("ps_r2", [1, 512])
    ps_tr = P("ps_tr", [128, 128], BF16)

    s = Sched()

    def ddump(nm, src_ap, deps):
        if _DBG and nm in dbg:
            dst = dbg.pop(nm)
            s.dma("gpsimd", (lambda e, dst=dst, src_ap=src_ap: e.dma_start(out=dst, in_=src_ap)),
                  deps=deps)

    pslast = {}
    buf_rd = {}

    row_off = [0]
    def row(n):
        o = row_off[0]
        row_off[0] += n
        assert row_off[0] <= 2560
        return rows[0:1, o:o + n]
    def row_reset(to):
        row_off[0] = to

    tok_cn = []
    tok_cn.append(s.dma("sync", lambda e: e.dma_start(out=mask[:], in_=maskc)))
    t_id = s.dma("sync", lambda e: e.dma_start(out=ident[:], in_=idc))
    tok_cn.append(t_id)
    tok_cn.append(s.dma("sync", lambda e: e.dma_start(out=battn_c[:], in_=battn_pc)))
    t_bv = s.dma("gpsimd", (lambda e: e.dma_start(
        out=bv_r[:], in_=b_attn[2048:3072].rearrange("(o f) -> o f", o=1))), sem="dvw")
    tok_cn.append(t_bv)
    tok_cn.append(s.dma("sync", lambda e: e.dma_start(out=bproj_c[:], in_=bproj_pc)))
    tok_cn.append(s.dma("sync", lambda e: e.dma_start(out=convw_c[:], in_=convw_pc)))
    tok_cn.append(s.dma("sync", lambda e: e.dma_start(out=convb_c[:], in_=convb_pc)))
    s.dve(lambda e: e.memset(ones_r[:], 1.0))
    s.dve(lambda e: e.memset(ones_rf[:], 1.0))
    s.dve(lambda e: e.memset(ones_c[:], 1.0))
    s.dve(lambda e: e.memset(ones_cf[:], 1.0))
    s.dve(lambda e: e.memset(
        vring[:].rearrange("p (r h d) -> p r h d", r=NRING, d=HD + 1)[:, :, :, HD:HD + 1], 1.0))
    s.dve(lambda e: e.memset(cols[:], 0.0))
    s.dve(lambda e: e.memset(cols[:, 31:32], EPS))
    _t = s.dve(lambda e: e.memset(cols[:, 30:31], 1.0), ms=True)
    _t = s.dve(lambda e: e.tensor_scalar(cols[:, 30:31], cols[:, 30:31], 2.0 ** -30, 1.0,
                                         ALU.add, ALU.subtract), deps=[_t], ms=True)
    t_ones = s.dve(lambda e: e.tensor_scalar(cols[:, 30:31], cols[:, 30:31], 2.0 ** 29, 256.0,
                                             ALU.mult, ALU.add), deps=[_t], ms=True)
    tok_cn.append(t_ones)

    wraw_last = [None, None]

    def wquant(which):
        src, dstT = (w1, w1T) if which == 0 else (w2, w2T)
        nrow = (HID if which == 0 else C) // 128
        ncol = C if which == 0 else HID
        nelem = float(HID * C)
        acc_col = cols[:, 0 + which:1 + which]
        red_col = cols[:, 2 + which:3 + which]
        sw_col = cols[:, 4 + which:5 + which]
        m_col = cols[:, 6 + which:7 + which]
        npc = ncol // 512

        prev = list(wraw_last)
        red_toks = []
        last2_guard = [None]
        for r_ in range(nrow):
            for p_ in range(npc):
                i = r_ * npc + p_
                slot = i % 2
                dst = wraw[:, slot * 512:(slot + 1) * 512]
                tok_d = s.dma("gpsimd", (lambda e, dst=dst, r_=r_, p_=p_: e.dma_start(
                    out=dst, in_=src[r_ * 128:(r_ + 1) * 128, p_ * 512:(p_ + 1) * 512])),
                    deps=([prev[slot]] if prev[slot] else []), sem="ws%d" % slot)
                if i == nrow * npc - 1:
                    tok_d = s.pool(lambda e: e.drain(), deps=[], ms=True)
                    last2_guard[0] = tok_d
                if i >= nrow * npc - 2:
                    tok_d = last2_guard[0] or tok_d
                tok_r = s.dve((lambda e, i=i, which=which, dst=dst: e.reduce_sum(
                    uscr[:, which * 64 + i: which * 64 + i + 1], dst,
                    axis=AX.X, apply_absolute_value=True)),
                    deps=[tok_d, t_ones], ms=True)
                prev[slot] = tok_r
                red_toks.append(tok_r)
        tok_a = s.dve((lambda e, which=which: e.reduce_sum(
            acc_col, uscr[:, which * 64:which * 64 + 64], axis=AX.X)),
            deps=red_toks[-4:], ms=True)
        tok_mm = s.pe(lambda e: e.matmul(ps_r1[0:1, 0:1], ones_cf[:, 0:1], acc_col,
                                         start=True, stop=True),
                      deps=[tok_a, t_ones, pslast.get("r1")], ms=True)
        r_pair = row(2)
        tok = s.dve(lambda e: e.tensor_scalar(r_pair[0:1, 0:1], ps_r1[0:1, 0:1],
                                              1.0 / nelem, EPS, ALU.mult, ALU.max),
                    deps=[tok_mm], ms=True)
        pslast["r1"] = tok
        tok = s.dve(lambda e: e.reciprocal(r_pair[0:1, 1:2], r_pair[0:1, 0:1]),
                    deps=[tok], ms=True)
        tok_b = s.pe(lambda e: e.matmul(ps_c[:, 0:2], ones_rf[0:1, :], r_pair,
                                        start=True, stop=True),
                     deps=[tok, pslast.get("c")], ms=True)
        tok_m = s.dve(lambda e: e.tensor_copy(m_col, ps_c[:, 0:1]), deps=[tok_b], ms=True)
        tok_sw = s.dve(lambda e: e.tensor_copy(sw_col, ps_c[:, 1:2]), deps=[tok_b], ms=True)
        pslast["c"] = tok_sw

        prev = [None, None]
        prev_tr = None
        tok_st = None
        for r_ in range(nrow):
            for pp in range(ncol // 1024):
                tok_q = None
                for hh in range(2):
                    slot = hh
                    dst = wraw[:, slot * 512:(slot + 1) * 512]
                    tok_d = s.dma("gpsimd", (lambda e, dst=dst, r_=r_, pp=pp, hh=hh: e.dma_start(
                        out=dst, in_=src[r_ * 128:(r_ + 1) * 128,
                                         pp * 1024 + hh * 512: pp * 1024 + (hh + 1) * 512])),
                        deps=([prev[slot]] if prev[slot] else []), sem="ws%d" % slot)
                    if r_ == nrow - 1 and pp == ncol // 1024 - 1 and hh == 1:
                        tok_d = s.pool(lambda e: e.drain(), deps=[], ms=True)
                    dq = [tok_d, tok_sw] + ([tok_q] if tok_q else [])
                    s.dve((lambda e, dst=dst: e.tensor_scalar(
                        iscr[:], dst, sw_col, cols[:, 30:31], ALU.mult, ALU.add)), deps=dq)
                    s.dve((lambda e, hh=hh: e.tensor_scalar(
                        wqs[:, hh * 512:(hh + 1) * 512], iscr[:], 256.0, 1.0,
                        ALU.subtract, ALU.min)))
                    tok_q = s.dve((lambda e, hh=hh: e.tensor_scalar(
                        wqs[:, hh * 512:(hh + 1) * 512], wqs[:, hh * 512:(hh + 1) * 512],
                        -1.0, None, ALU.max)), ms=True)
                    prev[slot] = tok_q
                for b_ in range(8):
                    dep_t = [tok_q, pslast.get("tr"), t_id] + ([tok_st] if tok_st else [])
                    tok_t = s.pe((lambda e, b_=b_: e.transpose(
                        ps_tr[:], wqs[:, b_ * 128:(b_ + 1) * 128], ident[:])),
                        deps=dep_t, ms=True)
                    last_tr_mm = tok_t
                    prev_tr = s.dve((lambda e, b_=b_: e.tensor_copy(
                        wts[:, b_ * 128:(b_ + 1) * 128], ps_tr[:])), deps=[tok_t], ms=True)
                    pslast["tr"] = prev_tr
                tok_st = s.dma("sync", (lambda e, r_=r_, pp=pp: e.dma_start(
                    out=dstT[pp * 1024:(pp + 1) * 1024, r_ * 128:(r_ + 1) * 128].rearrange(
                        "(b p) h -> p b h", p=128),
                    in_=wts[:, 0:1024].rearrange("p (b h) -> p b h", h=128))),
                    deps=[prev_tr] + ([tok_st] if tok_st else []), sem="dst")
        return tok_st, m_col

    tok_w1st, m1_col = wquant(0)
    tok_w2st, _ = wquant(1)
    if _DBG:
        _wq_deps = [t for t in (tok_w1st, tok_w2st) if t is not None] or [s.milestone("vector")]
        dw1 = nc.dram_tensor("dbg_w1T", [C, HID], BF16, kind="ExternalOutput").ap()
        dw2 = nc.dram_tensor("dbg_w2T", [HID, C], BF16, kind="ExternalOutput").ap()
        dcl = nc.dram_tensor("dbg_cols", [128, 32], F32, kind="ExternalOutput").ap()
        dus = nc.dram_tensor("dbg_us", [128, SUB], F32, kind="ExternalOutput").ap()
        dwi = nc.dram_tensor("dbg_w1in", [256, C], F32, kind="ExternalOutput").ap()
        dwj = nc.dram_tensor("dbg_w2in", [256, HID], F32, kind="ExternalOutput").ap()
        s.dma("gpsimd", lambda e: e.dma_start(out=dw1, in_=w1T), deps=_wq_deps)
        s.dma("gpsimd", lambda e: e.dma_start(out=dw2, in_=w2T), deps=_wq_deps)
        s.dma("gpsimd", lambda e: e.dma_start(out=dcl, in_=cols[:]), deps=_wq_deps)
        s.dma("gpsimd", lambda e: e.dma_start(out=dus, in_=uscr[:]), deps=_wq_deps)
        s.dma("gpsimd", lambda e: e.dma_start(out=dwi, in_=w1[HID - 256:HID, :]), deps=_wq_deps)
        s.dma("gpsimd", lambda e: e.dma_start(out=dwj, in_=w2[C - 256:C, :]), deps=_wq_deps)
    try:
        _cut("wquant")
    except _StopBuild:
        return _finish(nc, ctxs, s)

    def bcast_row(row_ap, out_sb, width, deps):
        tok_m = s.pe((lambda e: e.matmul(ps_c[:, 0:width], ones_rf[0:1, :], row_ap,
                                         start=True, stop=True)),
                     deps=list(deps) + [t_ones, pslast.get("c")], ms=True)
        tok = s.dve((lambda e: e.tensor_copy(out_sb, ps_c[:, 0:width])), deps=[tok_m], ms=True)
        pslast["c"] = tok
        return tok

    def col_to_row(col_ap, row_ap, n, deps):
        tok = None
        for b_ in range(n // 128):
            tok_t = s.pe((lambda e, b_=b_: e.transpose(
                ps_tr[0:1, 0:128], col_ap[:, b_:b_ + 1], ident[:])),
                deps=list(deps) + [pslast.get("tr"), t_id], ms=True)
            tok = s.dve((lambda e, b_=b_: e.tensor_copy(
                row_ap[0:1, b_ * 128:(b_ + 1) * 128], ps_tr[0:1, 0:128])),
                deps=[tok_t], ms=True)
            pslast["tr"] = tok
        return tok

    def ln_stats(tile_ap, width, deps):
        r_mean = row(width)
        r_sq = row(width)
        tok_ev = []
        for hf in range(width // 512):
            mm1_deps = list(deps) + [t_ones, pslast.get("r1")]
            for c_ in range(NCH):
                ap = tile_ap(c_)
                s.pe((lambda e, ap=ap, hf=hf, c_=c_: e.matmul(
                    ps_r1[0:1, :], ones_c[:, 0:1], ap[:, hf * 512:(hf + 1) * 512],
                    start=(c_ == 0), stop=(c_ == NCH - 1))), deps=mm1_deps)
            tok_mm1 = s.milestone("tensor")
            sq_deps = list(deps)
            for c_ in range(NCH):
                ap = tile_ap(c_)
                tok_s = s.act((lambda e, ap=ap, hf=hf: e.activation(
                    sqs[:], ap[:, hf * 512:(hf + 1) * 512], AF.Square)),
                    deps=sq_deps, ms=True)
                s.pe((lambda e, c_=c_: e.matmul(
                    ps_r2[0:1, :], ones_c[:, 0:1], sqs[:],
                    start=(c_ == 0), stop=(c_ == NCH - 1))),
                    deps=[tok_s, pslast.get("r2"), t_ones])
                tok_mm2 = s.milestone("tensor")
                sq_deps = [tok_mm2]
            tok1 = s.dve((lambda e, hf=hf: e.tensor_scalar(
                r_mean[0:1, hf * 512:(hf + 1) * 512], ps_r1[0:1, :], 1.0 / C, None, ALU.mult)),
                deps=[tok_mm1], ms=True)
            tok2 = s.dve((lambda e, hf=hf: e.tensor_scalar(
                r_sq[0:1, hf * 512:(hf + 1) * 512], ps_r2[0:1, :], 1.0 / C, None, ALU.mult)),
                deps=[tok_mm2], ms=True)
            pslast["r1"] = tok1
            pslast["r2"] = tok2
            tok_ev += [tok1, tok2]
        r_var = row(width)
        tok = s.dve(lambda e: e.scalar_tensor_tensor(
            r_var, r_mean, 1.0, r_mean, ALU.mult, ALU.mult), deps=tok_ev, ms=True)
        tok = s.dve(lambda e: e.tensor_sub(r_var, r_sq, r_var), deps=[tok], ms=True)
        r_rstd = row(width)
        tok = s.act(lambda e: e.activation(r_rstd, r_var, AF.Ln, bias=cols[0:1, 31:32]), deps=[tok, t_ones], ms=True)
        tok = s.act(lambda e: e.activation(r_rstd, r_rstd, AF.Exp, scale=-0.5), deps=[tok], ms=True)
        return r_mean, r_rstd, tok

    kv_wr = {}
    kv_rd = {}
    ering_rd = {}
    x1_wr = {}
    x1_rd = {}
    h_wr = {}
    h_rd = {}
    t_startup = [tok_w1st, tok_w2st] + tok_cn
    xq_rd = [None]
    xw_rd = [None]
    q_rd = [None]
    rc_rd = [None]
    y2_rd = [None]
    prev_xload = [None]
    out_toks = []

    def pieces_of(Q):
        out = []
        for j in range(max(0, Q // 128 - 2), Q // 128 + 4):
            lo, hi = 128 * j, 128 * j + EW
            plo, phi = max(lo, Q), min(hi, Q + 512, T)
            if phi > plo:
                out.append((j, plo, phi))
        return out

    def quant(k):
        ks3 = k % 3
        x1ap = lambda c_: x1ring[:, (ks3 * NCH + c_) * 512:(ks3 * NCH + c_ + 1) * 512]
        row_reset(0)
        m2_row, r2_row, tok_ln2 = ln_stats(x1ap, 512, x1_wr[k])
        tok_m2 = bcast_row(m2_row, m2bc[:], 512, [tok_ln2])
        d_toks = []
        tok_acc = None
        for c_ in range(NCH):
            dd = xw_t[:, c_ * 512:(c_ + 1) * 512]
            dep = [tok_m2, xw_rd[0]] + x1_wr[k]
            tok_d = s.dve((lambda e, c_=c_, dd=dd: e.tensor_sub(dd, x1ap(c_), m2bc[:])),
                          deps=dep, ms=True)
            d_toks.append(tok_d)
            if c_ == 0:
                tok_acc = s.dve((lambda e, dd=dd: e.scalar_tensor_tensor(
                    accs[:], dd, -1.0, dd, ALU.mult, ALU.max)), deps=[tok_d], ms=True)
            else:
                s.dve((lambda e, dd=dd: e.scalar_tensor_tensor(
                    sqs[:], dd, -1.0, accs[:], ALU.mult, ALU.max)), deps=[tok_d])
                tok_acc = s.dve((lambda e, dd=dd: e.tensor_max(accs[:], dd, sqs[:])), ms=True)
        tok_cols = []
        for b_ in range(4):
            tok_t = s.pe((lambda e, b_=b_: e.transpose(
                ps_tr[:], accs[:, b_ * 128:(b_ + 1) * 128], ident[:])),
                deps=[tok_acc, pslast.get("tr"), t_id], ms=True)
            tok_r = s.dve((lambda e, b_=b_: e.reduce_max(
                colsb[:, b_:b_ + 1], ps_tr[:], axis=AX.X)), deps=[tok_t], ms=True)
            pslast["tr"] = tok_r
            tok_cols.append(tok_r)
        r_maxd = rows[0:1, 2048:2560]
        tok_row = col_to_row(colsb[:, 0:4], r_maxd, 512, tok_cols)
        r_a2 = rows[0:1, 512:1024]
        tok = s.dve(lambda e: e.tensor_mul(r_a2, r_maxd, r2_row), deps=[tok_row, tok_ln2], ms=True)
        tok = s.dve(lambda e: e.tensor_scalar(r_a2, r_a2, EPS, None, ALU.max), deps=[tok], ms=True)
        r_rc = rows[0:1, 1024:1536]
        tok_rc = s.dve(lambda e: e.reciprocal(r_rc, r_a2), deps=[tok], ms=True)
        r_rs1 = rows[0:1, 0:512]
        tok_rs1 = s.dve(lambda e: e.scalar_tensor_tensor(
            r_rs1, r_rc, 127.0, r2_row, ALU.mult, ALU.mult), deps=[tok_rc], ms=True)
        r_hs1 = rows[0:1, 2048:2560]
        tok_hs1 = s.dve(lambda e: e.tensor_scalar(
            r_hs1, r_a2, m1_col[0:1, 0:1], 1.0 / 127.0, ALU.mult, ALU.mult),
            deps=[tok], ms=True)
        tok_rs1b = bcast_row(r_rs1, rs1bc[:], 512, [tok_rs1])
        tok_hs1b = bcast_row(r_hs1, hs1bc[:], 512, [tok_hs1])
        xq_toks = []
        for c_ in range(NCH):
            dd = xw_t[:, c_ * 512:(c_ + 1) * 512]
            dep = [tok_rs1b, d_toks[c_], xq_rd[0]]
            s.dve((lambda e, dd=dd: e.tensor_mul(uscr[:], dd, rs1bc[:])), deps=dep)
            s.dve(lambda e: e.tensor_scalar(iscr[:], uscr[:], cols[:, 30:31], None, ALU.add))
            tok = s.dve((lambda e, c_=c_: e.tensor_scalar(
                xqt[:, c_ * 512:(c_ + 1) * 512], iscr[:], 256.0, None, ALU.subtract)), ms=True)
            xq_toks.append(tok)
        xw_rd[0] = xq_toks[-1]
        return xq_toks, tok_hs1b

    def w1_mm(k, xq_toks, tok_hs1b):
        sl = k % 2
        h_toks = []
        last_mm = None
        for hc in range(HCH):
            wslot = hc % 2
            wdst = wdb[:, wslot * 1024:(wslot + 1) * 1024]
            tok_w = s.dma("sync", (lambda e, hc=hc, wdst=wdst: e.dma_start(
                out=wdst.rearrange("p (b h) -> p b h", h=128),
                in_=w1T[:, hc * 128:(hc + 1) * 128].rearrange(
                    "(b p) h -> p b h", p=128))),
                deps=[buf_rd.get("wdb%d" % wslot), tok_w1st], sem="dwb%d" % wslot)
            dep_mm = [tok_w, pslast.get("a%d" % wslot)] + xq_toks + (h_rd.get((sl, hc)) or [])
            for fc in range(NCH):
                s.pe((lambda e, fc=fc, wslot=wslot, wdst=wdst: e.matmul(
                    ps_a[wslot][:], wdst[:, fc * 128:(fc + 1) * 128],
                    xqt[:, fc * 512:(fc + 1) * 512],
                    start=(fc == 0), stop=(fc == NCH - 1))), deps=dep_mm)
            tok_mm = s.milestone("tensor")
            buf_rd["wdb%d" % wslot] = tok_mm
            last_mm = tok_mm
            tok_e = s.dve((lambda e, sl=sl, hc=hc, wslot=wslot: e.tensor_mul(
                h2[:, (sl * HCH + hc) * 514 + 1:(sl * HCH + hc) * 514 + 513],
                ps_a[wslot][:], hs1bc[:])), deps=[tok_mm, tok_hs1b], ms=True)
            pslast["a%d" % wslot] = tok_e
            h_toks.append(tok_e)
        h_wr[k] = h_toks
        xq_rd[0] = last_mm
        return h_toks

    def conv_block(k, seq_last_sub):
        sl = k % 2
        sln = (k + 1) % 2
        ks_in_seq = k % 4
        sq_i = k // 4
        base = lambda hc: (sl * HCH + hc) * 514
        basen = lambda hc: (sln * HCH + hc) * 514
        hview = lambda slx: h2[:, slx * HCH * 514:(slx + 1) * HCH * 514].rearrange(
            "p (c w) -> p c w", w=514)

        halo = []
        if ks_in_seq == 0:
            halo.append(s.dve(lambda e: e.memset(hview(sl)[:, :, 0:1], 0.0),
                              deps=h_wr[k], ms=True))
        if seq_last_sub:
            halo.append(s.dve(lambda e: e.memset(hview(sl)[:, :, 513:514], 0.0),
                              deps=h_wr[k], ms=True))
        else:
            halo.append(s.dve(lambda e: e.tensor_copy(
                hview(sl)[:, :, 513:514], hview(sln)[:, :, 1:2]),
                deps=h_wr[k] + h_wr.get(k + 1, []), ms=True))
            halo.append(s.dve(lambda e: e.tensor_copy(
                hview(sln)[:, :, 0:1], hview(sl)[:, :, 512:513]),
                deps=h_wr[k] + h_wr.get(k + 1, []), ms=True))

        sh_toks = []
        tok_acc = None
        prev_sh = None
        for hc in range(HCH):
            b0 = base(hc)
            dep = halo + [h_wr[k][hc]] + ([prev_sh] if prev_sh else [])
            s.dve((lambda e, b0=b0, hc=hc: e.tensor_scalar(
                cs1[:], h2[:, b0:b0 + 512], convw_c[:, 3 * hc:3 * hc + 1], None, ALU.mult)),
                deps=dep)
            s.dve((lambda e, b0=b0, hc=hc: e.scalar_tensor_tensor(
                cs2[:], h2[:, b0 + 1:b0 + 513], convw_c[:, 3 * hc + 1:3 * hc + 2],
                cs1[:], ALU.mult, ALU.add)))
            tok_cv = s.dve((lambda e, b0=b0, hc=hc: e.scalar_tensor_tensor(
                cs1[:], h2[:, b0 + 2:b0 + 514], convw_c[:, 3 * hc + 2:3 * hc + 3],
                cs2[:], ALU.mult, ALU.add)), ms=True)
            tok_sh = s.act((lambda e, b0=b0, hc=hc: e.activation(
                h2[:, b0 + 1:b0 + 513], cs1[:], AF.Silu,
                bias=convb_c[:, hc:hc + 1])), deps=[tok_cv], ms=True)
            prev_sh = tok_sh
            sh_toks.append(tok_sh)
            if hc == 0:
                tok_acc = s.dve((lambda e, b0=b0: e.scalar_tensor_tensor(
                    accs[:], h2[:, b0 + 1:b0 + 513], -1.0, h2[:, b0 + 1:b0 + 513],
                    ALU.mult, ALU.max)), deps=[tok_sh], ms=True)
            else:
                s.dve((lambda e, b0=b0: e.scalar_tensor_tensor(
                    sqs[:], h2[:, b0 + 1:b0 + 513], -1.0, accs[:],
                    ALU.mult, ALU.max)), deps=[tok_sh])
                tok_acc = s.dve((lambda e, b0=b0: e.tensor_max(
                    accs[:], h2[:, b0 + 1:b0 + 513], sqs[:])), ms=True)

        tok_cols = []
        for b_ in range(4):
            tok_t = s.pe((lambda e, b_=b_: e.transpose(
                ps_tr[:], accs[:, b_ * 128:(b_ + 1) * 128], ident[:])),
                deps=[tok_acc, pslast.get("tr"), t_id], ms=True)
            tok_r = s.dve((lambda e, b_=b_: e.reduce_max(
                colsb[:, 4 + b_:5 + b_], ps_tr[:], axis=AX.X)), deps=[tok_t], ms=True)
            pslast["tr"] = tok_r
            tok_cols.append(tok_r)
        r_maxs = rows[0:1, 0:512]
        tok_row = col_to_row(colsb[:, 4:8], r_maxs, 512, tok_cols)
        r_s2c = rows[0:1, 512:1024]
        tok = s.dve(lambda e: e.tensor_scalar(r_s2c, r_maxs, EPS, None, ALU.max),
                    deps=[tok_row], ms=True)
        r_rc2 = rows[0:1, 1024:1536]
        tok = s.dve(lambda e: e.reciprocal(r_rc2, r_s2c), deps=[tok], ms=True)
        tok = s.dve(lambda e: e.tensor_scalar(r_rc2, r_rc2, 127.0, None, ALU.mult),
                    deps=[tok], ms=True)
        tok_s2b = bcast_row(r_rc2, rs1bc[:], 512, [tok, xw_rd[0]])

        hq_toks = []
        for hc in range(HCH):
            b0 = base(hc)
            dep = [tok_s2b, sh_toks[hc], tok_acc]
            s.dve((lambda e, b0=b0: e.tensor_mul(uscr[:], h2[:, b0 + 1:b0 + 513], rs1bc[:])),
                  deps=dep)
            s.dve(lambda e: e.tensor_scalar(iscr[:], uscr[:], cols[:, 30:31], None, ALU.add))
            tok = s.dve((lambda e, b0=b0: e.tensor_scalar(
                h2[:, b0 + 1:b0 + 513], iscr[:], 256.0, None, ALU.subtract)), ms=True)
            hq_toks.append(tok)

        y2_toks = []
        last_mm = None
        for dc in range(NCH):
            hw_tok = []
            for hhf in range(2):
                rs = (dc * 2 + hhf) % 3
                wdst = w2db[:, rs * 2048:(rs + 1) * 2048]
                tok_w = s.dma("sync", (lambda e, dc=dc, hhf=hhf, wdst=wdst: e.dma_start(
                    out=wdst.rearrange("p (b h) -> p b h", h=128),
                    in_=w2T[hhf * 2048:(hhf + 1) * 2048,
                            dc * 128:(dc + 1) * 128].rearrange(
                        "(b p) h -> p b h", p=128))),
                    deps=[buf_rd.get("w2h%d" % rs), tok_w2st], sem="dh%d" % rs)
                hw_tok.append((rs, tok_w))
            w2slot = dc % 2
            dep_mm = [t for _, t in hw_tok] + [pslast.get("a%d" % w2slot)] + hq_toks
            for hc in range(HCH):
                b0 = base(hc)
                rs, _ = hw_tok[hc // 16]
                wdst2 = w2db[:, rs * 2048:(rs + 1) * 2048]
                s.pe((lambda e, hc=hc, w2slot=w2slot, wdst2=wdst2, b0=b0: e.matmul(
                    ps_a[w2slot][:], wdst2[:, (hc % 16) * 128:((hc % 16) + 1) * 128],
                    h2[:, b0 + 1:b0 + 513],
                    start=(hc == 0), stop=(hc == HCH - 1))), deps=dep_mm)
            tok_mm = s.milestone("tensor")
            for rs, _ in hw_tok:
                buf_rd["w2h%d" % rs] = tok_mm
            last_mm = tok_mm
            tok_e = s.dve((lambda e, dc=dc, w2slot=w2slot: e.tensor_copy(
                y2sb[:, dc * 512:(dc + 1) * 512], ps_a[w2slot][:])),
                deps=[tok_mm, y2_rd[0]], ms=True)
            pslast["a%d" % w2slot] = tok_e
            y2_toks.append(tok_e)
        if k == 0:
            ddump("dbg_y2", y2sb[:], y2_toks)
        for hc in range(HCH):
            h_rd[(sl, hc)] = [last_mm]

        row_reset(0)
        m3_row, r3_row, tok_ln3 = ln_stats(
            lambda c_: y2sb[:, c_ * 512:(c_ + 1) * 512], 512, y2_toks)
        tok_m3 = bcast_row(m3_row, m3bc[:], 512, [tok_ln3])
        tok_r3 = bcast_row(r3_row, r3bc[:], 512, [tok_ln3])
        ks3 = k % 3
        x1ap = lambda c_: x1ring[:, (ks3 * NCH + c_) * 512:(ks3 * NCH + c_ + 1) * 512]
        prev_out = {}
        for c_ in range(NCH):
            stg = outstg[:, (c_ % 2) * 512:(c_ % 2 + 1) * 512]
            dep = [tok_m3, tok_r3, y2_toks[c_]] + x1_wr[k] + [prev_out.get(c_ % 2)]
            s.dve((lambda e, c_=c_, stg=stg: e.tensor_sub(
                cs1[:], y2sb[:, c_ * 512:(c_ + 1) * 512], m3bc[:])), deps=dep)
            s.dve(lambda e: e.tensor_mul(cs1[:], cs1[:], r3bc[:]))
            tok_o = s.dve((lambda e, c_=c_, stg=stg: e.tensor_add(
                stg, cs1[:], x1ap(c_))), ms=True)
            tok_d = s.dma("sync", (lambda e, c_=c_, stg=stg, sq_i=sq_i, k=k: e.dma_start(
                out=out_t[sq_i, c_ * 128:(c_ + 1) * 128,
                          (k % 4) * 512:(k % 4 + 1) * 512],
                in_=stg)), deps=[tok_o], sem="dou%d" % (c_ % 2))
            prev_out[c_ % 2] = tok_d
            out_toks.append(tok_d)
        x1_rd[k] = tok_o
        y2_rd[0] = tok_o

    def _tiles():
     for tt in range(4):
        sq_i = tt // 2
        hf = tt % 2
        L0 = hf * TILE
        seq_first = hf == 0
        seq_last = hf == 1

        def ringblk(b):
            return b % NRING

        for half in range(2):
            h0 = half * 512
            Q = L0 + h0
            k_sub = tt * 2 + half

            tok_x = s.dma("gpsimd", (lambda e, sq_i=sq_i, Q=Q: e.dma_start(
                out=xb_t[:, 0:NCH * 512].rearrange("p (c t) -> p c t", t=512),
                in_=x_in[sq_i, :, Q:Q + 512].rearrange("(c p) t -> p c t", p=128))),
                deps=t_startup + [prev_xload[0]] + (h_rd.get((0, 0)) or []), sem="dxb")
            xb_ch = lambda c_: xb_t[:, c_ * 512:(c_ + 1) * 512]

            row_reset(4)
            m_row, r_row, tok_ln1 = ln_stats(xb_ch, 512, [tok_x])

            tok_mb = bcast_row(m_row, mrbc[:], 512, [tok_ln1])
            tok_rb = bcast_row(r_row, rrbc[:], 512, [tok_ln1])
            xn_toks = []
            for c_ in range(NCH):
                dst = xw_t[:, c_ * 512:(c_ + 1) * 512]
                s.dve((lambda e, c_=c_, dst=dst: e.tensor_sub(
                    dst, xb_ch(c_), mrbc[:])),
                    deps=[tok_mb, tok_x, xw_rd[0]])
                tok = s.dve((lambda e, dst=dst: e.tensor_mul(dst, dst, rrbc[:])),
                            deps=[tok_rb], ms=True)
                xn_toks.append(tok)
            if tt == 0 and half == 0:
                ddump("dbg_xn", xw_t[:], xn_toks)

            q_wr = {}
            last_xn_reader = None
            for oc in range(16):
                slot = oc % 2
                wdst = wdb[:, slot * 1024:(slot + 1) * 1024]
                tok_w = s.dma("gpsimd", (lambda e, oc=oc, wdst=wdst: e.dma_start(
                    out=wdst.rearrange("p (f o) -> p f o", o=128),
                    in_=w_attn[:, oc * 128:(oc + 1) * 128].rearrange(
                        "(f p) o -> p f o", p=128))),
                    deps=[buf_rd.get("wdb%d" % slot)] + t_startup, sem="dwa%d" % slot)
                dep_mm = [tok_w, pslast.get("a%d" % slot)] + xn_toks
                for fc in range(NCH):
                    s.pe((lambda e, fc=fc, slot=slot, wdst=wdst: e.matmul(
                        ps_a[slot][:], wdst[:, fc * 128:(fc + 1) * 128],
                        xw_t[:, fc * 512:(fc + 1) * 512],
                        start=(fc == 0), stop=(fc == NCH - 1))), deps=dep_mm)
                tok_mm = s.milestone("tensor")
                buf_rd["wdb%d" % slot] = tok_mm
                last_xn_reader = tok_mm
                if oc < 8:
                    tok_e = s.dve((lambda e, oc=oc, slot=slot: e.tensor_scalar(
                        q_t[:, oc * 512:(oc + 1) * 512],
                        ps_a[slot][:], battn_c[:, oc:oc + 1], None, ALU.add)),
                        deps=[tok_mm, q_rd[0], y2_rd[0]], ms=True)
                    q_wr[oc] = tok_e
                else:
                    kc = oc - 8
                    dep_e = [tok_mm] + [kv_rd.get(ringblk(hf * 8 + half * 4 + bi))
                                        for bi in range(4)]
                    tok_e = None
                    for bi in range(4):
                        blk = ringblk(hf * 8 + half * 4 + bi)
                        tok_e = s.dve((lambda e, kc=kc, slot=slot, bi=bi, blk=blk, oc=oc:
                                       e.tensor_scalar(
                                           kring[:, (kc * NRING + blk) * 128:(kc * NRING + blk) * 128 + 128],
                                           ps_a[slot][:, bi * 128:(bi + 1) * 128],
                                           battn_c[:, oc:oc + 1], None, ALU.add)),
                                      deps=dep_e, ms=True)
                        kv_wr[("k", blk, kc)] = tok_e
                pslast["a%d" % slot] = tok_e

            for chf in range(2):
                vw_toks = []
                for fc in range(NCH):
                    wdst = w2db[:, fc * 512:(fc + 1) * 512]
                    dep = [buf_rd.get("vw%d" % fc), buf_rd.get("w2h0"), buf_rd.get("w2h1")]
                    tok_vw = s.dma("gpsimd", (lambda e, fc=fc, chf=chf, wdst=wdst: e.dma_start(
                        out=wdst, in_=w_attn[fc * 128:(fc + 1) * 128,
                                            2048 + chf * 512:2048 + (chf + 1) * 512])),
                        deps=dep + t_startup, sem="dvw")
                    vw_toks.append(tok_vw)
                for tc4 in range(4):
                    blk = ringblk(hf * 8 + half * 4 + tc4)
                    slot = tc4 % 2
                    dep_mm = xn_toks + [pslast.get("a%d" % slot), t_bv] + vw_toks
                    for fc in range(NCH):
                        s.pe((lambda e, fc=fc, tc4=tc4, slot=slot: e.matmul(
                            ps_a[slot][:],
                            xw_t[:, fc * 512 + tc4 * 128: fc * 512 + tc4 * 128 + 128],
                            w2db[:, fc * 512:(fc + 1) * 512],
                            start=(fc == 0), stop=False)), deps=dep_mm)
                    s.pe((lambda e, chf=chf, slot=slot: e.matmul(
                        ps_a[slot][:], ones_r[0:1, :],
                        bv_r[0:1, chf * 512:(chf + 1) * 512],
                        start=False, stop=True)))
                    tok_mm = s.milestone("tensor")
                    last_xn_reader = tok_mm
                    buf_rd["w2h0"] = tok_mm
                    buf_rd["w2h1"] = tok_mm
                    for fc in range(NCH):
                        buf_rd["vw%d" % fc] = tok_mm
                    dep_e = [tok_mm, kv_rd.get(("v", blk))]
                    tok_e = s.dve((lambda e, blk=blk, chf=chf, slot=slot: e.tensor_copy(
                        vring[:, blk * VS + chf * 8 * 65: blk * VS + (chf * 8 + 8) * 65].rearrange(
                            "p (h d) -> p h d", d=65)[:, :, 0:HD],
                        ps_a[slot][:].rearrange("p (h d) -> p h d", d=HD))), deps=dep_e, ms=True)
                    kv_wr[("v", blk, chf)] = tok_e
                    pslast["a%d" % slot] = tok_e
            if tt == 0 and half == 0:
                ddump("dbg_q", q_t[:], [q_wr.get(i) for i in range(8)])
                ddump("dbg_k", kring[:], [kv_wr.get(("k", b, kc)) for b in range(4) for kc in range(8)])
                ddump("dbg_v", vring[:], [kv_wr.get(("v", b, cf)) for b in range(4) for cf in range(2)])
            xw_rd[0] = last_xn_reader
            _cut("qkv%d_%d" % (tt, half))

            pcs = pieces_of(Q)
            jmin = pcs[0][0]
            ecum = {}
            _c = 0
            for (j, plo, phi) in pcs:
                ecum[j] = _c
                _c += phi - plo
            o_toks = []
            last_score_mm = None
            for grp in range(NH // HG):
                for (j, plo, phi) in pcs:
                    w = phi - plo
                    moff = plo - 128 * j
                    blk = ringblk(j)
                    for hi_ in range(HG):
                        h = grp * HG + hi_
                        ch = h // 2
                        pb = 64 * (h % 2)
                        psl = ps_a[hi_]
                        eslot = (j - jmin) * HG + hi_
                        ecol = hi_ * 1536 + ecum[j]
                        dep_mm = [kv_wr.get(("k", blk, ch)), q_wr.get(ch),
                                  pslast.get("a%d" % hi_), ering_rd.get(eslot)]
                        tok_mm = s.pe((lambda e, ch=ch, pb=pb, blk=blk, plo=plo, w=w, psl=psl, Q=Q:
                                       e.matmul(psl[:, 0:w],
                                                kring[pb:pb + 64, (ch * NRING + blk) * 128:(ch * NRING + blk) * 128 + 128],
                                                q_t[pb:pb + 64, ch * 512 + (plo - Q): ch * 512 + (plo - Q) + w],
                                                start=True, stop=True)), deps=dep_mm, ms=True)
                        last_score_mm = tok_mm
                        tok_ex = s.act((lambda e, ecol=ecol, psl=psl, w=w: e.activation(
                            ering[:, ecol:ecol + w], psl[:, 0:w], AF.Exp, scale=0.125)),
                            deps=[tok_mm], ms=True)
                        pslast["a%d" % hi_] = tok_ex
                        tok_k = s.dve((lambda e, ecol=ecol, moff=moff, w=w: e.tensor_mul(
                            ering[:, ecol:ecol + w], ering[:, ecol:ecol + w],
                            mask[:, moff:moff + w])), deps=[tok_ex], ms=True)
                        kv_wr[("e", eslot)] = tok_k
                for hi_ in range(HG):
                    h = grp * HG + hi_
                    psl = ps_b[hi_]
                    deps0 = [pslast.get("b%d" % hi_)]
                    for (j, plo, phi) in pcs:
                        deps0 += [kv_wr.get(("e", (j - jmin) * HG + hi_)),
                                  kv_wr.get(("v", ringblk(j), 0)),
                                  kv_wr.get(("v", ringblk(j), 1))]
                    first = True
                    for (j, plo, phi) in pcs:
                        blk = ringblk(j)
                        ecol = hi_ * 1536 + ecum[j]
                        s.pe((lambda e, blk=blk, h=h, plo=plo, phi=phi, ecol=ecol, psl=psl, Q=Q, first=first:
                              e.matmul(psl[0:65, plo - Q:phi - Q],
                                       vring[:, blk * VS + h * 65: blk * VS + (h + 1) * 65],
                                       ering[:, ecol:ecol + (phi - plo)],
                                       start=first, stop=False)), deps=deps0)
                        first = False
                    tok_mm = s.milestone("tensor")
                    for (j, plo, phi) in pcs:
                        ering_rd[(j - jmin) * HG + hi_] = tok_mm
                        kv_rd[("v", ringblk(j))] = tok_mm
                        kv_rd[ringblk(j)] = tok_mm
                    ob = osb[:, 0:512]
                    tok_c = s.dve((lambda e, psl=psl, ob=ob: e.tensor_copy(ob, psl[0:65, :])),
                                  deps=[tok_mm], ms=True)
                    rc = rcr[0:1, 2048:2560]
                    tok_r = s.dve((lambda e, psl=psl, rc=rc: e.reciprocal(rc, psl[64:65, :])),
                                  deps=[tok_mm, rc_rd[0]], ms=True)
                    pslast["b%d" % hi_] = tok_r
                    tok_b = s.pe((lambda e, rc=rc: e.matmul(
                        ps_c[0:64, :], ones_rf[0:1, 0:64], rc, start=True, stop=True)),
                        deps=[tok_r, pslast.get("c")], ms=True)
                    rc_rd[0] = tok_b
                    bcs = bcsb[0:64, 0:512]
                    tok_bc = s.dve((lambda e, bcs=bcs: e.tensor_copy(bcs, ps_c[0:64, :])),
                                   deps=[tok_b], ms=True)
                    pslast["c"] = tok_bc
                    ch = h // 2
                    pb = 64 * (h % 2)
                    tok_o = s.dve((lambda e, ch=ch, pb=pb, ob=ob, bcs=bcs: e.tensor_mul(
                        xw_t[pb:pb + 64, ch * 512:(ch + 1) * 512], ob[0:64, :], bcs)),
                        deps=[tok_bc, xw_rd[0]], ms=True)
                    o_toks.append(tok_o)
            if tt == 0 and half == 0:
                ddump("dbg_o", xw_t[:], o_toks)
                ddump("dbg_e", ering[:], o_toks)
            q_rd[0] = last_score_mm
            _cut("av%d_%d" % (tt, half))

            pj_toks = []
            last_o_reader = None
            for cc in range(NCH):
                slot = cc % 2
                wdst = wdb[:, slot * 1024:(slot + 1) * 1024]
                tok_w = s.dma("gpsimd", (lambda e, cc=cc, wdst=wdst: e.dma_start(
                    out=wdst.rearrange("p (f o) -> p f o", o=128),
                    in_=w_proj[:, cc * 128:(cc + 1) * 128].rearrange(
                        "(f p) o -> p f o", p=128))),
                    deps=[buf_rd.get("wdb%d" % slot)], sem="dwa%d" % slot)
                dep_mm = [tok_w, pslast.get("a%d" % slot)] + o_toks
                for fc in range(NCH):
                    s.pe((lambda e, fc=fc, slot=slot, wdst=wdst: e.matmul(
                        ps_a[slot][:], wdst[:, fc * 128:(fc + 1) * 128],
                        xw_t[:, fc * 512:(fc + 1) * 512],
                        start=(fc == 0), stop=(fc == NCH - 1))), deps=dep_mm)
                tok_mm = s.milestone("tensor")
                buf_rd["wdb%d" % slot] = tok_mm
                last_o_reader = tok_mm
                tok_e = s.dve((lambda e, cc=cc, slot=slot: e.tensor_scalar(
                    y2sb[:, cc * 512:(cc + 1) * 512], ps_a[slot][:],
                    bproj_c[:, cc:cc + 1], None, ALU.add)),
                    deps=[tok_mm, y2_rd[0], q_rd[0]], ms=True)
                pslast["a%d" % slot] = tok_e
                pj_toks.append(tok_e)
            if tt == 0 and half == 0:
                ddump("dbg_pj", y2sb[:], pj_toks)
            xw_rd[0] = last_o_reader

            row_reset(0)
            pm_row, pr_row, tok_lnp = ln_stats(
                lambda c_: y2sb[:, c_ * 512:(c_ + 1) * 512], 512, pj_toks)
            tok_mb3 = bcast_row(pm_row, m3bc[:], 512, [tok_lnp])
            tok_rb3 = bcast_row(pr_row, r3bc[:], 512, [tok_lnp])
            ks3 = k_sub % 3
            x1_toks = []
            for c_ in range(NCH):
                x1dst = x1ring[:, (ks3 * NCH + c_) * 512:(ks3 * NCH + c_ + 1) * 512]
                dep = [tok_mb3, tok_rb3, pj_toks[c_], x1_rd.get(k_sub - 3)]
                s.dve((lambda e, c_=c_, x1dst=x1dst: e.tensor_sub(
                    x1dst, y2sb[:, c_ * 512:(c_ + 1) * 512], m3bc[:])), deps=dep)
                s.dve((lambda e, x1dst=x1dst: e.tensor_mul(x1dst, x1dst, r3bc[:])))
                tok = s.dve((lambda e, c_=c_, x1dst=x1dst: e.tensor_add(
                    x1dst, x1dst, xb_ch(c_))), deps=[tok_x], ms=True)
                x1_toks.append(tok)
            if tt == 0 and half == 0:
                ddump("dbg_x1", x1ring[:, 0:NCH * SUB], x1_toks)
            x1_wr[k_sub] = x1_toks
            y2_rd[0] = x1_toks[-1]
            prev_xload[0] = x1_toks[-1]
            _cut("x1_%d_%d" % (tt, half))
        _cut("attn%d" % tt)
        kA, kB = 2 * tt, 2 * tt + 1
        xqA, hsA = quant(kA)
        if tt == 0:
            ddump("dbg_xq", xqt[:], xqA)
        _cut("quantA%d" % tt)
        toksA = w1_mm(kA, xqA, hsA)
        if tt == 0:
            ddump("dbg_h", h2[:, 0:HCH * 514], toksA)
        _cut("w1A%d" % tt)
        if not seq_first:
            conv_block(kA - 1, seq_last_sub=False)
        xqB, hsB = quant(kB)
        w1_mm(kB, xqB, hsB)
        conv_block(kA, seq_last_sub=False)
        if seq_last:
            conv_block(kB, seq_last_sub=True)
        _cut("tile%d" % tt)

    try:
        _tiles()
    except _StopBuild:
        pass

    return _finish(nc, ctxs, s)


def _finish(nc, ctxs, s):
    final_deps = [(nm, s.count[nm]) for nm in Sched.DMA_SEMS if nm in s.count]
    final_deps += [("pe", s.count.get("pe", 0)), ("dve", s.count.get("dve", 0)),
                   ("act", s.count.get("act", 0))]
    s._op("gpsimd", lambda e: e.nop(), final_deps)

    sem_names = ["pe", "dve", "act", "pool"] + Sched.DMA_SEMS
    sem_ctxs = {}
    sems = {}
    for n in sem_names:
        g = nc.semaphore(n)
        sems[n] = g.__enter__()
        sem_ctxs[n] = g
    s.emit_all(nc, sems)
    for n in reversed(sem_names):
        sem_ctxs[n].__exit__(None, None, None)
    for g in reversed(ctxs):
        g.__exit__(None, None, None)
    return nc


_NC_CACHE = [None]


def _host_constants():
    i = np.arange(128)[:, None]
    jj = np.arange(EW)[None, :]
    band = ((jj >= i) & (jj <= i + WIN)).astype(ml_dtypes.bfloat16)
    ident = np.eye(128, dtype=np.float32).astype(ml_dtypes.bfloat16)
    return band, ident


def kernel(L0_attn_x, L0_attn_W_attn, L0_attn_b_attn, L0_attn_W_proj, L0_attn_b_proj,
           L0_ffn_w1, L0_ffn_conv_w, L0_ffn_conv_b, L0_ffn_w2):
    x = np.ascontiguousarray(L0_attn_x, dtype=np.float32)
    band, ident = _host_constants()
    ba = np.asarray(L0_attn_b_attn, np.float32)
    bp = np.asarray(L0_attn_b_proj, np.float32)
    cw = np.asarray(L0_ffn_conv_w, np.float32).reshape(HID, 3)
    cb = np.asarray(L0_ffn_conv_b, np.float32)
    base = {
        "w_attn": np.ascontiguousarray(L0_attn_W_attn, np.float32),
        "b_attn": np.ascontiguousarray(ba),
        "w_proj": np.ascontiguousarray(L0_attn_W_proj, np.float32),
        "b_proj": np.ascontiguousarray(bp),
        "w1": np.ascontiguousarray(L0_ffn_w1, np.float32),
        "convw": np.ascontiguousarray(cw),
        "convb": np.ascontiguousarray(cb),
        "w2": np.ascontiguousarray(L0_ffn_w2, np.float32),
        "maskc": band,
        "idc": ident,
        "battn_pc": np.ascontiguousarray(ba.reshape(3 * NCH, 128).T),
        "bproj_pc": np.ascontiguousarray(bp.reshape(NCH, 128).T),
        "convw_pc": np.ascontiguousarray(cw.reshape(HCH, 128, 3).transpose(1, 0, 2).reshape(128, 3 * HCH)),
        "convb_pc": np.ascontiguousarray(cb.reshape(HCH, 128).T),
    }
    in_maps = []
    for core in range(8):
        shard = x[core * SEQ:(core + 1) * SEQ]
        xt = np.ascontiguousarray(np.transpose(shard, (0, 2, 1)))
        m = dict(base)
        m["x_in"] = xt
        in_maps.append(m)

    if _NC_CACHE[0] is None:
        _NC_CACHE[0] = build_program()
    nc = _NC_CACHE[0]
    res = run_bass_kernel_spmd(nc, in_maps, core_ids=list(range(8)))
    outs = []
    for core in range(8):
        ot = res.results[core]["out_t"]
        outs.append(np.transpose(ot, (0, 2, 1)))
    return np.ascontiguousarray(np.concatenate(outs, axis=0), np.float32)


if __name__ == "__main__":
    build_program()
    print("build ok")


# version 02
# speedup vs baseline: 1.0219x; 1.0219x over previous
import os
import numpy as np
import ml_dtypes
import concourse.bass as bass
import concourse.mybir as mybir
from concourse.bass_utils import run_bass_kernel_spmd

F32 = mybir.dt.float32
BF16 = mybir.dt.bfloat16
I32 = mybir.dt.int32
AF = mybir.ActivationFunctionType
ALU = mybir.AluOpType
AX = mybir.AxisListType

SEQ = 2
T = 2048
C = 1024
NH = 16
HD = 64
WIN = 256
HID = 4096
TILE = 1024
SUB = 512
NCH = C // 128
HCH = HID // 128
NRING = 10
HG = 2
EW = 384
EPS = 1e-5
VS = NH * (HD + 1)


class Sched:
    def __init__(self):
        self.items = []
        self.count = {}
        self.waited = {}
        self.last_item = {}

    def _op(self, eng, emit, deps):
        waits = []
        for tok in deps:
            if tok is None:
                continue
            sem, val = tok
            if self.waited.get((eng, sem), 0) < val:
                waits.append((sem, val))
                self.waited[(eng, sem)] = val
        self.items.append([eng, emit, waits, []])
        self.last_item[eng] = len(self.items) - 1

    def pe(self, emit, deps=(), ms=False):
        self._op("tensor", emit, deps)
        return self.milestone("tensor") if ms else None

    def dve(self, emit, deps=(), ms=False):
        self._op("vector", emit, deps)
        return self.milestone("vector") if ms else None

    def act(self, emit, deps=(), ms=False):
        self._op("scalar", emit, deps)
        return self.milestone("scalar") if ms else None

    def pool(self, emit, deps=(), ms=False):
        self._op("gpsimd", emit, deps)
        return self.milestone("gpsimd") if ms else None

    DMA_SEMS = ["dsy", "dgp", "ws0", "ws1", "dst", "dxb", "dwa0", "dwa1",
                "dvw", "dwb0", "dwb1", "dh0", "dh1", "dh2", "dou0", "dou1"]

    def dma(self, q, emit, deps=(), sem=None):
        if sem is None:
            sem = "dsy" if q == "sync" else "dgp"
        self._op(q, emit, deps)
        self.count[sem] = self.count.get(sem, 0) + 16
        self.items[-1][3].append((sem, 16))
        return (sem, self.count[sem])

    def wait_only(self, eng, emit, deps):
        self._op(eng, emit, deps)

    def milestone(self, eng):
        name = {"tensor": "pe", "vector": "dve", "scalar": "act", "gpsimd": "pool"}[eng]
        idx = self.last_item[eng]
        self.count[name] = self.count.get(name, 0) + 1
        self.items[idx][3].append((name, 1))
        return (name, self.count[name])

    def emit_all(self, nc, sems):
        with nc.Block() as block:
            def mk(eng_key):
                def body(eng):
                    for it_eng, emit, waits, incs in self.items:
                        if it_eng != eng_key:
                            continue
                        for sem, val in waits:
                            eng.wait_ge(sems[sem], val)
                        inst = emit(eng)
                        for sem, amt in incs:
                            inst.then_inc(sems[sem], amt)
                return body
            block.tensor(mk("tensor"))
            block.vector(mk("vector"))
            block.scalar(mk("scalar"))
            block.gpsimd(mk("gpsimd"))
            block.sync(mk("sync"))


class _StopBuild(Exception):
    pass


_TRUNC = os.environ.get("KBUILD_TRUNC", "")
_DBG = bool(os.environ.get("KBUILD_DEBUG"))


def _cut(stage):
    if _TRUNC == stage:
        raise _StopBuild()


def build_program():
    nc = bass.Bass("TRN2", target_bir_lowering=False)

    x_in = nc.dram_tensor("x_in", [SEQ, C, T], F32, kind="ExternalInput").ap()
    w_attn = nc.dram_tensor("w_attn", [C, 3 * C], F32, kind="ExternalInput").ap()
    b_attn = nc.dram_tensor("b_attn", [3 * C], F32, kind="ExternalInput").ap()
    w_proj = nc.dram_tensor("w_proj", [C, C], F32, kind="ExternalInput").ap()
    b_proj = nc.dram_tensor("b_proj", [C], F32, kind="ExternalInput").ap()
    w1 = nc.dram_tensor("w1", [HID, C], F32, kind="ExternalInput").ap()
    convw = nc.dram_tensor("convw", [HID, 3], F32, kind="ExternalInput").ap()
    convb = nc.dram_tensor("convb", [HID], F32, kind="ExternalInput").ap()
    w2 = nc.dram_tensor("w2", [C, HID], F32, kind="ExternalInput").ap()
    maskc = nc.dram_tensor("maskc", [128, EW], BF16, kind="ExternalInput").ap()
    battn_pc = nc.dram_tensor("battn_pc", [128, 3 * NCH], F32, kind="ExternalInput").ap()
    bproj_pc = nc.dram_tensor("bproj_pc", [128, NCH], F32, kind="ExternalInput").ap()
    convw_pc = nc.dram_tensor("convw_pc", [128, 3 * HCH], F32, kind="ExternalInput").ap()
    convb_pc = nc.dram_tensor("convb_pc", [128, HCH], F32, kind="ExternalInput").ap()
    idc = nc.dram_tensor("idc", [128, 128], BF16, kind="ExternalInput").ap()

    out_t = nc.dram_tensor("out_t", [SEQ, C, T], F32, kind="ExternalOutput").ap()

    dbg = {}
    if _DBG:
        for nm, sh, dt in [("dbg_xn", [128, NCH * SUB], BF16), ("dbg_q", [128, NCH * SUB], BF16),
                           ("dbg_k", [128, NCH * NRING * 128], BF16), ("dbg_v", [128, NRING * VS], BF16),
                           ("dbg_o", [128, NCH * SUB], BF16), ("dbg_pj", [128, NCH * SUB], BF16),
                           ("dbg_x1", [128, NCH * SUB], BF16), ("dbg_xq", [128, NCH * SUB], BF16),
                           ("dbg_h", [128, HCH * 514], BF16), ("dbg_hq", [128, HCH * 514], BF16),
                           ("dbg_y2", [128, NCH * SUB], BF16), ("dbg_e", [128, HG * 1536], BF16)]:
            dbg[nm] = nc.dram_tensor(nm, sh, dt, kind="ExternalOutput").ap()

    w1T = nc.dram_tensor("w1T_scr", [C, HID], BF16, kind="Internal").ap()
    w2T = nc.dram_tensor("w2T_scr", [HID, C], BF16, kind="Internal").ap()
    rbounce = nc.dram_tensor("rbounce", [2 * T], F32, kind="Internal").ap()

    ctxs = []

    def S(name, shape, dt):
        g = nc.sbuf_tensor(name, shape, dt)
        t = g.__enter__()
        ctxs.append(g)
        return t

    q_t = S("q_t", [128, NCH * SUB], BF16)
    xw_t = S("xw_t", [128, NCH * SUB], BF16)
    kring = S("kring", [128, NCH * NRING * 128], BF16)
    vring = S("vring", [128, NRING * VS], BF16)
    ering = S("ering", [128, HG * 1536], BF16)
    x1ring = S("x1ring", [128, 3 * NCH * SUB], BF16)
    h2 = S("h2", [128, 2 * HCH * 514], BF16)
    wdb = S("wdb", [128, 2 * 1024], BF16)
    w2db = S("w2db", [128, 3 * 2048], BF16)
    xqt = S("xqt", [128, NCH * SUB], BF16)
    accs = S("accs", [128, SUB], BF16)
    cs1 = S("cs1", [128, SUB], BF16)
    cs2 = S("cs2", [128, SUB], BF16)
    uscr = S("uscr", [128, SUB], F32)
    iscr = S("iscr", [128, SUB], I32)
    osb = S("osb", [65, SUB], BF16)
    mask = S("mask", [128, EW], BF16)
    ident = S("ident", [128, 128], BF16)
    ones_r = S("ones_r", [1, 128], BF16)
    ones_rf = S("ones_rf", [1, 128], F32)
    ones_c = S("ones_c", [128, 1], BF16)
    ones_cf = S("ones_cf", [128, 1], F32)
    battn_c = S("battn_c", [128, 3 * NCH], F32)
    bv_r = S("bv_r", [1, C], BF16)
    bproj_c = S("bproj_c", [128, NCH], F32)
    convw_c = S("convw_c", [128, 3 * HCH], F32)
    convb_c = S("convb_c", [128, HCH], F32)
    rows = S("rows", [1, 2560], F32)
    mrbc = S("mrbc", [128, SUB], BF16)
    rrbc = S("rrbc", [128, SUB], BF16)
    hs1bc = S("hs1bc", [128, SUB], F32)
    rs1bc = S("rs1bc", [128, SUB], F32)
    cols = S("cols", [128, 32], F32)
    colsb = S("colsb", [128, 8], BF16)
    outstg = S("outstg", [128, 2 * SUB], F32)
    y2sb = q_t
    wqs = xqt
    wts = x1ring
    wraw = outstg
    sqs = cs2
    rcr = rows
    bcsb = accs
    m3bc = mrbc
    r3bc = rrbc
    m2bc = mrbc
    xb_t = h2

    def P(name, shape, dt=F32):
        g = nc.psum_tensor(name, shape, dt)
        t = g.__enter__()
        ctxs.append(g)
        return t

    ps_a = [P("ps_a0", [128, 512]), P("ps_a1", [128, 512])]
    ps_b = [P("ps_b0", [65, 512]), P("ps_b1", [65, 512])]
    ps_c = P("ps_c", [128, 512])
    ps_r1 = P("ps_r1", [1, 512])
    ps_r2 = P("ps_r2", [1, 512])
    ps_tr = P("ps_tr", [128, 128], BF16)

    s = Sched()

    def ddump(nm, src_ap, deps):
        if _DBG and nm in dbg:
            dst = dbg.pop(nm)
            s.dma("gpsimd", (lambda e, dst=dst, src_ap=src_ap: e.dma_start(out=dst, in_=src_ap)),
                  deps=deps)

    pslast = {}
    buf_rd = {}

    row_off = [0]
    def row(n):
        o = row_off[0]
        row_off[0] += n
        assert row_off[0] <= 2560
        return rows[0:1, o:o + n]
    def row_reset(to):
        row_off[0] = to

    tok_cn = []
    tok_cn.append(s.dma("sync", lambda e: e.dma_start(out=mask[:], in_=maskc)))
    t_id = s.dma("sync", lambda e: e.dma_start(out=ident[:], in_=idc))
    tok_cn.append(t_id)
    tok_cn.append(s.dma("sync", lambda e: e.dma_start(out=battn_c[:], in_=battn_pc)))
    t_bv = s.dma("gpsimd", (lambda e: e.dma_start(
        out=bv_r[:], in_=b_attn[2048:3072].rearrange("(o f) -> o f", o=1))), sem="dvw")
    tok_cn.append(t_bv)
    tok_cn.append(s.dma("sync", lambda e: e.dma_start(out=bproj_c[:], in_=bproj_pc)))
    tok_cn.append(s.dma("sync", lambda e: e.dma_start(out=convw_c[:], in_=convw_pc)))
    tok_cn.append(s.dma("sync", lambda e: e.dma_start(out=convb_c[:], in_=convb_pc)))
    s.dve(lambda e: e.memset(ones_r[:], 1.0))
    s.dve(lambda e: e.memset(ones_rf[:], 1.0))
    s.dve(lambda e: e.memset(ones_c[:], 1.0))
    s.dve(lambda e: e.memset(ones_cf[:], 1.0))
    s.dve(lambda e: e.memset(
        vring[:].rearrange("p (r h d) -> p r h d", r=NRING, d=HD + 1)[:, :, :, HD:HD + 1], 1.0))
    s.dve(lambda e: e.memset(cols[:], 0.0))
    s.dve(lambda e: e.memset(cols[:, 31:32], EPS))
    _t = s.dve(lambda e: e.memset(cols[:, 30:31], 1.0), ms=True)
    _t = s.dve(lambda e: e.tensor_scalar(cols[:, 30:31], cols[:, 30:31], 2.0 ** -30, 1.0,
                                         ALU.add, ALU.subtract), deps=[_t], ms=True)
    t_ones = s.dve(lambda e: e.tensor_scalar(cols[:, 30:31], cols[:, 30:31], 2.0 ** 29, 256.0,
                                             ALU.mult, ALU.add), deps=[_t], ms=True)
    tok_cn.append(t_ones)

    wraw_last = [None, None]

    def wquant(which):
        src, dstT = (w1, w1T) if which == 0 else (w2, w2T)
        nrow = (HID if which == 0 else C) // 128
        ncol = C if which == 0 else HID
        nelem = float(HID * C)
        acc_col = cols[:, 0 + which:1 + which]
        red_col = cols[:, 2 + which:3 + which]
        sw_col = cols[:, 4 + which:5 + which]
        m_col = cols[:, 6 + which:7 + which]
        npc = ncol // 512

        prev = list(wraw_last)
        red_toks = []
        last2_guard = [None]
        for r_ in range(nrow):
            for p_ in range(npc):
                i = r_ * npc + p_
                slot = i % 2
                dst = wraw[:, slot * 512:(slot + 1) * 512]
                tok_d = s.dma("gpsimd", (lambda e, dst=dst, r_=r_, p_=p_: e.dma_start(
                    out=dst, in_=src[r_ * 128:(r_ + 1) * 128, p_ * 512:(p_ + 1) * 512])),
                    deps=([prev[slot]] if prev[slot] else []), sem="ws%d" % slot)
                if i == nrow * npc - 1:
                    tok_d = s.pool(lambda e: e.drain(), deps=[], ms=True)
                    last2_guard[0] = tok_d
                if i >= nrow * npc - 2:
                    tok_d = last2_guard[0] or tok_d
                tok_r = s.dve((lambda e, i=i, which=which, dst=dst: e.reduce_sum(
                    uscr[:, which * 64 + i: which * 64 + i + 1], dst,
                    axis=AX.X, apply_absolute_value=True)),
                    deps=[tok_d, t_ones], ms=True)
                prev[slot] = tok_r
                red_toks.append(tok_r)
        tok_a = s.dve((lambda e, which=which: e.reduce_sum(
            acc_col, uscr[:, which * 64:which * 64 + 64], axis=AX.X)),
            deps=red_toks[-4:], ms=True)
        tok_mm = s.pe(lambda e: e.matmul(ps_r1[0:1, 0:1], ones_cf[:, 0:1], acc_col,
                                         start=True, stop=True),
                      deps=[tok_a, t_ones, pslast.get("r1")], ms=True)
        r_pair = row(2)
        tok = s.dve(lambda e: e.tensor_scalar(r_pair[0:1, 0:1], ps_r1[0:1, 0:1],
                                              1.0 / nelem, EPS, ALU.mult, ALU.max),
                    deps=[tok_mm], ms=True)
        pslast["r1"] = tok
        tok = s.dve(lambda e: e.reciprocal(r_pair[0:1, 1:2], r_pair[0:1, 0:1]),
                    deps=[tok], ms=True)
        tok_b = s.pe(lambda e: e.matmul(ps_c[:, 0:2], ones_rf[0:1, :], r_pair,
                                        start=True, stop=True),
                     deps=[tok, pslast.get("c")], ms=True)
        tok_m = s.dve(lambda e: e.tensor_copy(m_col, ps_c[:, 0:1]), deps=[tok_b], ms=True)
        tok_sw = s.dve(lambda e: e.tensor_copy(sw_col, ps_c[:, 1:2]), deps=[tok_b], ms=True)
        pslast["c"] = tok_sw

        prev = [None, None]
        prev_tr = None
        tok_st = None
        for r_ in range(nrow):
            for pp in range(ncol // 1024):
                tok_q = None
                for hh in range(2):
                    slot = hh
                    dst = wraw[:, slot * 512:(slot + 1) * 512]
                    tok_d = s.dma("gpsimd", (lambda e, dst=dst, r_=r_, pp=pp, hh=hh: e.dma_start(
                        out=dst, in_=src[r_ * 128:(r_ + 1) * 128,
                                         pp * 1024 + hh * 512: pp * 1024 + (hh + 1) * 512])),
                        deps=([prev[slot]] if prev[slot] else []), sem="ws%d" % slot)
                    if r_ == nrow - 1 and pp == ncol // 1024 - 1 and hh == 1:
                        tok_d = s.pool(lambda e: e.drain(), deps=[], ms=True)
                    dq = [tok_d, tok_sw] + ([tok_q] if tok_q else [])
                    s.dve((lambda e, dst=dst: e.tensor_scalar(
                        iscr[:], dst, sw_col, cols[:, 30:31], ALU.mult, ALU.add)), deps=dq)
                    s.dve((lambda e, hh=hh: e.tensor_scalar(
                        wqs[:, hh * 512:(hh + 1) * 512], iscr[:], 256.0, 1.0,
                        ALU.subtract, ALU.min)))
                    tok_q = s.dve((lambda e, hh=hh: e.tensor_scalar(
                        wqs[:, hh * 512:(hh + 1) * 512], wqs[:, hh * 512:(hh + 1) * 512],
                        -1.0, None, ALU.max)), ms=True)
                    prev[slot] = tok_q
                for b_ in range(8):
                    dep_t = [tok_q, pslast.get("tr"), t_id] + ([tok_st] if tok_st else [])
                    tok_t = s.pe((lambda e, b_=b_: e.transpose(
                        ps_tr[:], wqs[:, b_ * 128:(b_ + 1) * 128], ident[:])),
                        deps=dep_t, ms=True)
                    last_tr_mm = tok_t
                    prev_tr = s.dve((lambda e, b_=b_: e.tensor_copy(
                        wts[:, b_ * 128:(b_ + 1) * 128], ps_tr[:])), deps=[tok_t], ms=True)
                    pslast["tr"] = prev_tr
                tok_st = s.dma("sync", (lambda e, r_=r_, pp=pp: e.dma_start(
                    out=dstT[pp * 1024:(pp + 1) * 1024, r_ * 128:(r_ + 1) * 128].rearrange(
                        "(b p) h -> p b h", p=128),
                    in_=wts[:, 0:1024].rearrange("p (b h) -> p b h", h=128))),
                    deps=[prev_tr] + ([tok_st] if tok_st else []), sem="dst")
        return tok_st, m_col

    tok_w1st, m1_col = wquant(0)
    tok_w2st, _ = wquant(1)
    if _DBG:
        _wq_deps = [t for t in (tok_w1st, tok_w2st) if t is not None] or [s.milestone("vector")]
        dw1 = nc.dram_tensor("dbg_w1T", [C, HID], BF16, kind="ExternalOutput").ap()
        dw2 = nc.dram_tensor("dbg_w2T", [HID, C], BF16, kind="ExternalOutput").ap()
        dcl = nc.dram_tensor("dbg_cols", [128, 32], F32, kind="ExternalOutput").ap()
        dus = nc.dram_tensor("dbg_us", [128, SUB], F32, kind="ExternalOutput").ap()
        dwi = nc.dram_tensor("dbg_w1in", [256, C], F32, kind="ExternalOutput").ap()
        dwj = nc.dram_tensor("dbg_w2in", [256, HID], F32, kind="ExternalOutput").ap()
        s.dma("gpsimd", lambda e: e.dma_start(out=dw1, in_=w1T), deps=_wq_deps)
        s.dma("gpsimd", lambda e: e.dma_start(out=dw2, in_=w2T), deps=_wq_deps)
        s.dma("gpsimd", lambda e: e.dma_start(out=dcl, in_=cols[:]), deps=_wq_deps)
        s.dma("gpsimd", lambda e: e.dma_start(out=dus, in_=uscr[:]), deps=_wq_deps)
        s.dma("gpsimd", lambda e: e.dma_start(out=dwi, in_=w1[HID - 256:HID, :]), deps=_wq_deps)
        s.dma("gpsimd", lambda e: e.dma_start(out=dwj, in_=w2[C - 256:C, :]), deps=_wq_deps)
    try:
        _cut("wquant")
    except _StopBuild:
        return _finish(nc, ctxs, s)

    def bcast_row(row_ap, out_sb, width, deps):
        tok_m = s.pe((lambda e: e.matmul(ps_c[:, 0:width], ones_rf[0:1, :], row_ap,
                                         start=True, stop=True)),
                     deps=list(deps) + [t_ones, pslast.get("c")], ms=True)
        tok = s.dve((lambda e: e.tensor_copy(out_sb, ps_c[:, 0:width])), deps=[tok_m], ms=True)
        pslast["c"] = tok
        return tok

    def col_to_row(col_ap, row_ap, n, deps):
        tok = None
        for b_ in range(n // 128):
            tok_t = s.pe((lambda e, b_=b_: e.transpose(
                ps_tr[0:1, 0:128], col_ap[:, b_:b_ + 1], ident[:])),
                deps=list(deps) + [pslast.get("tr"), t_id], ms=True)
            tok = s.dve((lambda e, b_=b_: e.tensor_copy(
                row_ap[0:1, b_ * 128:(b_ + 1) * 128], ps_tr[0:1, 0:128])),
                deps=[tok_t], ms=True)
            pslast["tr"] = tok
        return tok

    def ln_stats(tile_ap, width, deps):
        r_mean = row(width)
        r_sq = row(width)
        tok_ev = []
        for hf in range(width // 512):
            mm1_deps = list(deps) + [t_ones, pslast.get("r1")]
            for c_ in range(NCH):
                ap = tile_ap(c_)
                s.pe((lambda e, ap=ap, hf=hf, c_=c_: e.matmul(
                    ps_r1[0:1, :], ones_c[:, 0:1], ap[:, hf * 512:(hf + 1) * 512],
                    start=(c_ == 0), stop=(c_ == NCH - 1))), deps=mm1_deps)
            tok_mm1 = s.milestone("tensor")
            sq_deps = list(deps)
            for c_ in range(NCH):
                ap = tile_ap(c_)
                tok_s = s.act((lambda e, ap=ap, hf=hf: e.activation(
                    sqs[:], ap[:, hf * 512:(hf + 1) * 512], AF.Square)),
                    deps=sq_deps, ms=True)
                s.pe((lambda e, c_=c_: e.matmul(
                    ps_r2[0:1, :], ones_c[:, 0:1], sqs[:],
                    start=(c_ == 0), stop=(c_ == NCH - 1))),
                    deps=[tok_s, pslast.get("r2"), t_ones])
                tok_mm2 = s.milestone("tensor")
                sq_deps = [tok_mm2]
            tok1 = s.dve((lambda e, hf=hf: e.tensor_scalar(
                r_mean[0:1, hf * 512:(hf + 1) * 512], ps_r1[0:1, :], 1.0 / C, None, ALU.mult)),
                deps=[tok_mm1], ms=True)
            tok2 = s.dve((lambda e, hf=hf: e.tensor_scalar(
                r_sq[0:1, hf * 512:(hf + 1) * 512], ps_r2[0:1, :], 1.0 / C, None, ALU.mult)),
                deps=[tok_mm2], ms=True)
            pslast["r1"] = tok1
            pslast["r2"] = tok2
            tok_ev += [tok1, tok2]
        r_var = row(width)
        tok = s.dve(lambda e: e.scalar_tensor_tensor(
            r_var, r_mean, 1.0, r_mean, ALU.mult, ALU.mult), deps=tok_ev, ms=True)
        tok = s.dve(lambda e: e.tensor_sub(r_var, r_sq, r_var), deps=[tok], ms=True)
        r_rstd = row(width)
        tok = s.act(lambda e: e.activation(r_rstd, r_var, AF.Ln, bias=cols[0:1, 31:32]), deps=[tok, t_ones], ms=True)
        tok = s.act(lambda e: e.activation(r_rstd, r_rstd, AF.Exp, scale=-0.5), deps=[tok], ms=True)
        return r_mean, r_rstd, tok

    kv_wr = {}
    kv_rd = {}
    ering_rd = {}
    x1_wr = {}
    x1_rd = {}
    h_wr = {}
    h_rd = {}
    t_startup = [tok_w1st, tok_w2st] + tok_cn
    xq_rd = [None]
    xw_rd = [None]
    q_rd = [None]
    rc_rd = [None]
    y2_rd = [None]
    prev_xload = [None]
    out_toks = []

    def pieces_of(Q):
        out = []
        for j in range(max(0, Q // 128 - 2), Q // 128 + 4):
            lo, hi = 128 * j, 128 * j + EW
            plo, phi = max(lo, Q), min(hi, Q + 512, T)
            if phi > plo:
                out.append((j, plo, phi))
        return out

    def quant(k):
        ks3 = k % 3
        x1ap = lambda c_: x1ring[:, (ks3 * NCH + c_) * 512:(ks3 * NCH + c_ + 1) * 512]
        row_reset(0)
        m2_row, r2_row, tok_ln2 = ln_stats(x1ap, 512, x1_wr[k])
        tok_m2 = bcast_row(m2_row, m2bc[:], 512, [tok_ln2])
        d_toks = []
        tok_acc = None
        for c_ in range(NCH):
            dd = xw_t[:, c_ * 512:(c_ + 1) * 512]
            dep = [tok_m2, xw_rd[0]] + x1_wr[k]
            tok_d = s.dve((lambda e, c_=c_, dd=dd: e.tensor_sub(dd, x1ap(c_), m2bc[:])),
                          deps=dep, ms=True)
            d_toks.append(tok_d)
            if c_ == 0:
                tok_acc = s.dve((lambda e, dd=dd: e.scalar_tensor_tensor(
                    accs[:], dd, -1.0, dd, ALU.mult, ALU.max)), deps=[tok_d], ms=True)
            else:
                s.dve((lambda e, dd=dd: e.scalar_tensor_tensor(
                    sqs[:], dd, -1.0, accs[:], ALU.mult, ALU.max)), deps=[tok_d])
                tok_acc = s.dve((lambda e, dd=dd: e.tensor_max(accs[:], dd, sqs[:])), ms=True)
        tok_cols = []
        for b_ in range(4):
            tok_t = s.pe((lambda e, b_=b_: e.transpose(
                ps_tr[:], accs[:, b_ * 128:(b_ + 1) * 128], ident[:])),
                deps=[tok_acc, pslast.get("tr"), t_id], ms=True)
            tok_r = s.dve((lambda e, b_=b_: e.reduce_max(
                colsb[:, b_:b_ + 1], ps_tr[:], axis=AX.X)), deps=[tok_t], ms=True)
            pslast["tr"] = tok_r
            tok_cols.append(tok_r)
        r_maxd = rows[0:1, 2048:2560]
        tok_row = col_to_row(colsb[:, 0:4], r_maxd, 512, tok_cols)
        r_a2 = rows[0:1, 512:1024]
        tok = s.dve(lambda e: e.tensor_mul(r_a2, r_maxd, r2_row), deps=[tok_row, tok_ln2], ms=True)
        tok = s.dve(lambda e: e.tensor_scalar(r_a2, r_a2, EPS, None, ALU.max), deps=[tok], ms=True)
        r_rc = rows[0:1, 1024:1536]
        tok_rc = s.act(lambda e: e.activation(r_rc, r_a2, AF.Ln), deps=[tok], ms=True)
        tok_rc = s.act(lambda e: e.activation(r_rc, r_rc, AF.Exp, scale=-1.0), deps=[tok_rc], ms=True)
        r_rs1 = rows[0:1, 0:512]
        tok_rs1 = s.dve(lambda e: e.scalar_tensor_tensor(
            r_rs1, r_rc, 127.0, r2_row, ALU.mult, ALU.mult), deps=[tok_rc], ms=True)
        r_hs1 = rows[0:1, 2048:2560]
        tok_hs1 = s.dve(lambda e: e.tensor_scalar(
            r_hs1, r_a2, m1_col[0:1, 0:1], 1.0 / 127.0, ALU.mult, ALU.mult),
            deps=[tok], ms=True)
        tok_rs1b = bcast_row(r_rs1, rs1bc[:], 512, [tok_rs1])
        tok_hs1b = bcast_row(r_hs1, hs1bc[:], 512, [tok_hs1])
        xq_toks = []
        for c_ in range(NCH):
            dd = xw_t[:, c_ * 512:(c_ + 1) * 512]
            dep = [tok_rs1b, d_toks[c_], xq_rd[0]]
            s.dve((lambda e, dd=dd: e.tensor_mul(uscr[:], dd, rs1bc[:])), deps=dep)
            s.dve(lambda e: e.tensor_scalar(iscr[:], uscr[:], cols[:, 30:31], None, ALU.add))
            tok = s.dve((lambda e, c_=c_: e.tensor_scalar(
                xqt[:, c_ * 512:(c_ + 1) * 512], iscr[:], 256.0, None, ALU.subtract)), ms=True)
            xq_toks.append(tok)
        xw_rd[0] = xq_toks[-1]
        return xq_toks, tok_hs1b

    def w1_mm(k, xq_toks, tok_hs1b):
        sl = k % 2
        h_toks = []
        last_mm = None
        for hc in range(HCH):
            wslot = hc % 2
            wdst = wdb[:, wslot * 1024:(wslot + 1) * 1024]
            tok_w = s.dma("sync", (lambda e, hc=hc, wdst=wdst: e.dma_start(
                out=wdst.rearrange("p (b h) -> p b h", h=128),
                in_=w1T[:, hc * 128:(hc + 1) * 128].rearrange(
                    "(b p) h -> p b h", p=128))),
                deps=[buf_rd.get("wdb%d" % wslot), tok_w1st], sem="dwb%d" % wslot)
            dep_mm = [tok_w, pslast.get("a%d" % wslot)] + xq_toks + (h_rd.get((sl, hc)) or [])
            for fc in range(NCH):
                s.pe((lambda e, fc=fc, wslot=wslot, wdst=wdst: e.matmul(
                    ps_a[wslot][:], wdst[:, fc * 128:(fc + 1) * 128],
                    xqt[:, fc * 512:(fc + 1) * 512],
                    start=(fc == 0), stop=(fc == NCH - 1))), deps=dep_mm)
            tok_mm = s.milestone("tensor")
            buf_rd["wdb%d" % wslot] = tok_mm
            last_mm = tok_mm
            tok_e = s.dve((lambda e, sl=sl, hc=hc, wslot=wslot: e.tensor_mul(
                h2[:, (sl * HCH + hc) * 514 + 1:(sl * HCH + hc) * 514 + 513],
                ps_a[wslot][:], hs1bc[:])), deps=[tok_mm, tok_hs1b], ms=True)
            pslast["a%d" % wslot] = tok_e
            h_toks.append(tok_e)
        h_wr[k] = h_toks
        xq_rd[0] = last_mm
        return h_toks

    def conv_block(k, seq_last_sub):
        sl = k % 2
        sln = (k + 1) % 2
        ks_in_seq = k % 4
        sq_i = k // 4
        base = lambda hc: (sl * HCH + hc) * 514
        basen = lambda hc: (sln * HCH + hc) * 514
        hview = lambda slx: h2[:, slx * HCH * 514:(slx + 1) * HCH * 514].rearrange(
            "p (c w) -> p c w", w=514)

        halo = []
        if ks_in_seq == 0:
            halo.append(s.dve(lambda e: e.memset(hview(sl)[:, :, 0:1], 0.0),
                              deps=h_wr[k], ms=True))
        if seq_last_sub:
            halo.append(s.dve(lambda e: e.memset(hview(sl)[:, :, 513:514], 0.0),
                              deps=h_wr[k], ms=True))
        else:
            halo.append(s.dve(lambda e: e.tensor_copy(
                hview(sl)[:, :, 513:514], hview(sln)[:, :, 1:2]),
                deps=h_wr[k] + h_wr.get(k + 1, []), ms=True))
            halo.append(s.dve(lambda e: e.tensor_copy(
                hview(sln)[:, :, 0:1], hview(sl)[:, :, 512:513]),
                deps=h_wr[k] + h_wr.get(k + 1, []), ms=True))

        sh_toks = []
        tok_acc = None
        prev_sh = None
        for hc in range(HCH):
            b0 = base(hc)
            dep = halo + [h_wr[k][hc]] + ([prev_sh] if prev_sh else [])
            s.dve((lambda e, b0=b0, hc=hc: e.tensor_scalar(
                cs1[:], h2[:, b0:b0 + 512], convw_c[:, 3 * hc:3 * hc + 1], None, ALU.mult)),
                deps=dep)
            s.dve((lambda e, b0=b0, hc=hc: e.scalar_tensor_tensor(
                cs2[:], h2[:, b0 + 1:b0 + 513], convw_c[:, 3 * hc + 1:3 * hc + 2],
                cs1[:], ALU.mult, ALU.add)))
            tok_cv = s.dve((lambda e, b0=b0, hc=hc: e.scalar_tensor_tensor(
                cs1[:], h2[:, b0 + 2:b0 + 514], convw_c[:, 3 * hc + 2:3 * hc + 3],
                cs2[:], ALU.mult, ALU.add)), ms=True)
            tok_sh = s.act((lambda e, b0=b0, hc=hc: e.activation(
                h2[:, b0 + 1:b0 + 513], cs1[:], AF.Silu,
                bias=convb_c[:, hc:hc + 1])), deps=[tok_cv], ms=True)
            prev_sh = tok_sh
            sh_toks.append(tok_sh)
            if hc == 0:
                tok_acc = s.dve((lambda e, b0=b0: e.scalar_tensor_tensor(
                    accs[:], h2[:, b0 + 1:b0 + 513], -1.0, h2[:, b0 + 1:b0 + 513],
                    ALU.mult, ALU.max)), deps=[tok_sh], ms=True)
            else:
                s.dve((lambda e, b0=b0: e.scalar_tensor_tensor(
                    sqs[:], h2[:, b0 + 1:b0 + 513], -1.0, accs[:],
                    ALU.mult, ALU.max)), deps=[tok_sh])
                tok_acc = s.dve((lambda e, b0=b0: e.tensor_max(
                    accs[:], h2[:, b0 + 1:b0 + 513], sqs[:])), ms=True)

        tok_cols = []
        for b_ in range(4):
            tok_t = s.pe((lambda e, b_=b_: e.transpose(
                ps_tr[:], accs[:, b_ * 128:(b_ + 1) * 128], ident[:])),
                deps=[tok_acc, pslast.get("tr"), t_id], ms=True)
            tok_r = s.dve((lambda e, b_=b_: e.reduce_max(
                colsb[:, 4 + b_:5 + b_], ps_tr[:], axis=AX.X)), deps=[tok_t], ms=True)
            pslast["tr"] = tok_r
            tok_cols.append(tok_r)
        r_maxs = rows[0:1, 0:512]
        tok_row = col_to_row(colsb[:, 4:8], r_maxs, 512, tok_cols)
        r_s2c = rows[0:1, 512:1024]
        tok = s.dve(lambda e: e.tensor_scalar(r_s2c, r_maxs, EPS, None, ALU.max),
                    deps=[tok_row], ms=True)
        r_rc2 = rows[0:1, 1024:1536]
        tok = s.act(lambda e: e.activation(r_rc2, r_s2c, AF.Ln), deps=[tok], ms=True)
        tok = s.act(lambda e: e.activation(r_rc2, r_rc2, AF.Exp, scale=-1.0), deps=[tok], ms=True)
        tok = s.dve(lambda e: e.tensor_scalar(r_rc2, r_rc2, 127.0, None, ALU.mult),
                    deps=[tok], ms=True)
        tok_s2b = bcast_row(r_rc2, rs1bc[:], 512, [tok, xw_rd[0]])

        hq_toks = []
        for hc in range(HCH):
            b0 = base(hc)
            dep = [tok_s2b, sh_toks[hc], tok_acc]
            s.dve((lambda e, b0=b0: e.tensor_mul(uscr[:], h2[:, b0 + 1:b0 + 513], rs1bc[:])),
                  deps=dep)
            s.dve(lambda e: e.tensor_scalar(iscr[:], uscr[:], cols[:, 30:31], None, ALU.add))
            tok = s.dve((lambda e, b0=b0: e.tensor_scalar(
                h2[:, b0 + 1:b0 + 513], iscr[:], 256.0, None, ALU.subtract)), ms=True)
            hq_toks.append(tok)

        y2_toks = []
        last_mm = None
        for dc in range(NCH):
            hw_tok = []
            for hhf in range(2):
                rs = (dc * 2 + hhf) % 3
                wdst = w2db[:, rs * 2048:(rs + 1) * 2048]
                tok_w = s.dma("sync", (lambda e, dc=dc, hhf=hhf, wdst=wdst: e.dma_start(
                    out=wdst.rearrange("p (b h) -> p b h", h=128),
                    in_=w2T[hhf * 2048:(hhf + 1) * 2048,
                            dc * 128:(dc + 1) * 128].rearrange(
                        "(b p) h -> p b h", p=128))),
                    deps=[buf_rd.get("w2h%d" % rs), tok_w2st], sem="dh%d" % rs)
                hw_tok.append((rs, tok_w))
            w2slot = dc % 2
            dep_mm = [t for _, t in hw_tok] + [pslast.get("a%d" % w2slot)] + hq_toks
            for hc in range(HCH):
                b0 = base(hc)
                rs, _ = hw_tok[hc // 16]
                wdst2 = w2db[:, rs * 2048:(rs + 1) * 2048]
                s.pe((lambda e, hc=hc, w2slot=w2slot, wdst2=wdst2, b0=b0: e.matmul(
                    ps_a[w2slot][:], wdst2[:, (hc % 16) * 128:((hc % 16) + 1) * 128],
                    h2[:, b0 + 1:b0 + 513],
                    start=(hc == 0), stop=(hc == HCH - 1))), deps=dep_mm)
            tok_mm = s.milestone("tensor")
            for rs, _ in hw_tok:
                buf_rd["w2h%d" % rs] = tok_mm
            last_mm = tok_mm
            tok_e = s.act((lambda e, dc=dc, w2slot=w2slot: e.copy(
                y2sb[:, dc * 512:(dc + 1) * 512], ps_a[w2slot][:])),
                deps=[tok_mm, y2_rd[0]], ms=True)
            pslast["a%d" % w2slot] = tok_e
            y2_toks.append(tok_e)
        if k == 0:
            ddump("dbg_y2", y2sb[:], y2_toks)
        for hc in range(HCH):
            h_rd[(sl, hc)] = [last_mm]

        row_reset(0)
        m3_row, r3_row, tok_ln3 = ln_stats(
            lambda c_: y2sb[:, c_ * 512:(c_ + 1) * 512], 512, y2_toks)
        tok_m3 = bcast_row(m3_row, m3bc[:], 512, [tok_ln3])
        tok_r3 = bcast_row(r3_row, r3bc[:], 512, [tok_ln3])
        ks3 = k % 3
        x1ap = lambda c_: x1ring[:, (ks3 * NCH + c_) * 512:(ks3 * NCH + c_ + 1) * 512]
        prev_out = {}
        for c_ in range(NCH):
            stg = outstg[:, (c_ % 2) * 512:(c_ % 2 + 1) * 512]
            dep = [tok_m3, tok_r3, y2_toks[c_]] + x1_wr[k] + [prev_out.get(c_ % 2)]
            s.dve((lambda e, c_=c_, stg=stg: e.tensor_sub(
                cs1[:], y2sb[:, c_ * 512:(c_ + 1) * 512], m3bc[:])), deps=dep)
            s.dve(lambda e: e.tensor_mul(cs1[:], cs1[:], r3bc[:]))
            tok_o = s.dve((lambda e, c_=c_, stg=stg: e.tensor_add(
                stg, cs1[:], x1ap(c_))), ms=True)
            tok_d = s.dma("sync", (lambda e, c_=c_, stg=stg, sq_i=sq_i, k=k: e.dma_start(
                out=out_t[sq_i, c_ * 128:(c_ + 1) * 128,
                          (k % 4) * 512:(k % 4 + 1) * 512],
                in_=stg)), deps=[tok_o], sem="dou%d" % (c_ % 2))
            prev_out[c_ % 2] = tok_d
            out_toks.append(tok_d)
        x1_rd[k] = tok_o
        y2_rd[0] = tok_o

    def _tiles():
     for tt in range(4):
        sq_i = tt // 2
        hf = tt % 2
        L0 = hf * TILE
        seq_first = hf == 0
        seq_last = hf == 1

        def ringblk(b):
            return b % NRING

        for half in range(2):
            h0 = half * 512
            Q = L0 + h0
            k_sub = tt * 2 + half

            tok_x = s.dma("gpsimd", (lambda e, sq_i=sq_i, Q=Q: e.dma_start(
                out=xb_t[:, 0:NCH * 512].rearrange("p (c t) -> p c t", t=512),
                in_=x_in[sq_i, :, Q:Q + 512].rearrange("(c p) t -> p c t", p=128))),
                deps=t_startup + [prev_xload[0]] + (h_rd.get((0, 0)) or []), sem="dxb")
            xb_ch = lambda c_: xb_t[:, c_ * 512:(c_ + 1) * 512]

            row_reset(4)
            m_row, r_row, tok_ln1 = ln_stats(xb_ch, 512, [tok_x])

            tok_mb = bcast_row(m_row, mrbc[:], 512, [tok_ln1])
            tok_rb = bcast_row(r_row, rrbc[:], 512, [tok_ln1])
            xn_toks = []
            for c_ in range(NCH):
                dst = xw_t[:, c_ * 512:(c_ + 1) * 512]
                s.dve((lambda e, c_=c_, dst=dst: e.tensor_sub(
                    dst, xb_ch(c_), mrbc[:])),
                    deps=[tok_mb, tok_x, xw_rd[0]])
                tok = s.dve((lambda e, dst=dst: e.tensor_mul(dst, dst, rrbc[:])),
                            deps=[tok_rb], ms=True)
                xn_toks.append(tok)
            if tt == 0 and half == 0:
                ddump("dbg_xn", xw_t[:], xn_toks)

            q_wr = {}
            last_xn_reader = None
            for oc in range(16):
                slot = oc % 2
                wdst = wdb[:, slot * 1024:(slot + 1) * 1024]
                tok_w = s.dma("gpsimd", (lambda e, oc=oc, wdst=wdst: e.dma_start(
                    out=wdst.rearrange("p (f o) -> p f o", o=128),
                    in_=w_attn[:, oc * 128:(oc + 1) * 128].rearrange(
                        "(f p) o -> p f o", p=128))),
                    deps=[buf_rd.get("wdb%d" % slot)] + t_startup, sem="dwa%d" % slot)
                dep_mm = [tok_w, pslast.get("a%d" % slot)] + xn_toks
                for fc in range(NCH):
                    s.pe((lambda e, fc=fc, slot=slot, wdst=wdst: e.matmul(
                        ps_a[slot][:], wdst[:, fc * 128:(fc + 1) * 128],
                        xw_t[:, fc * 512:(fc + 1) * 512],
                        start=(fc == 0), stop=(fc == NCH - 1))), deps=dep_mm)
                tok_mm = s.milestone("tensor")
                buf_rd["wdb%d" % slot] = tok_mm
                last_xn_reader = tok_mm
                if oc < 8:
                    tok_e = s.act((lambda e, oc=oc, slot=slot: e.activation(
                        q_t[:, oc * 512:(oc + 1) * 512],
                        ps_a[slot][:], AF.Identity, bias=battn_c[:, oc:oc + 1])),
                        deps=[tok_mm, q_rd[0], y2_rd[0]], ms=True)
                    q_wr[oc] = tok_e
                else:
                    kc = oc - 8
                    dep_e = [tok_mm] + [kv_rd.get(ringblk(hf * 8 + half * 4 + bi))
                                        for bi in range(4)]
                    tok_e = None
                    for bi in range(4):
                        blk = ringblk(hf * 8 + half * 4 + bi)
                        tok_e = s.act((lambda e, kc=kc, slot=slot, bi=bi, blk=blk, oc=oc:
                                       e.activation(
                                           kring[:, (kc * NRING + blk) * 128:(kc * NRING + blk) * 128 + 128],
                                           ps_a[slot][:, bi * 128:(bi + 1) * 128],
                                           AF.Identity, bias=battn_c[:, oc:oc + 1])),
                                      deps=dep_e, ms=True)
                        kv_wr[("k", blk, kc)] = tok_e
                pslast["a%d" % slot] = tok_e

            for chf in range(2):
                vw_toks = []
                for fc in range(NCH):
                    wdst = w2db[:, fc * 512:(fc + 1) * 512]
                    dep = [buf_rd.get("vw%d" % fc), buf_rd.get("w2h0"), buf_rd.get("w2h1")]
                    tok_vw = s.dma("gpsimd", (lambda e, fc=fc, chf=chf, wdst=wdst: e.dma_start(
                        out=wdst, in_=w_attn[fc * 128:(fc + 1) * 128,
                                            2048 + chf * 512:2048 + (chf + 1) * 512])),
                        deps=dep + t_startup, sem="dvw")
                    vw_toks.append(tok_vw)
                for tc4 in range(4):
                    blk = ringblk(hf * 8 + half * 4 + tc4)
                    slot = tc4 % 2
                    dep_mm = xn_toks + [pslast.get("a%d" % slot), t_bv] + vw_toks
                    for fc in range(NCH):
                        s.pe((lambda e, fc=fc, tc4=tc4, slot=slot: e.matmul(
                            ps_a[slot][:],
                            xw_t[:, fc * 512 + tc4 * 128: fc * 512 + tc4 * 128 + 128],
                            w2db[:, fc * 512:(fc + 1) * 512],
                            start=(fc == 0), stop=False)), deps=dep_mm)
                    s.pe((lambda e, chf=chf, slot=slot: e.matmul(
                        ps_a[slot][:], ones_r[0:1, :],
                        bv_r[0:1, chf * 512:(chf + 1) * 512],
                        start=False, stop=True)))
                    tok_mm = s.milestone("tensor")
                    last_xn_reader = tok_mm
                    buf_rd["w2h0"] = tok_mm
                    buf_rd["w2h1"] = tok_mm
                    for fc in range(NCH):
                        buf_rd["vw%d" % fc] = tok_mm
                    dep_e = [tok_mm, kv_rd.get(("v", blk))]
                    tok_e = s.dve((lambda e, blk=blk, chf=chf, slot=slot: e.tensor_copy(
                        vring[:, blk * VS + chf * 8 * 65: blk * VS + (chf * 8 + 8) * 65].rearrange(
                            "p (h d) -> p h d", d=65)[:, :, 0:HD],
                        ps_a[slot][:].rearrange("p (h d) -> p h d", d=HD))), deps=dep_e, ms=True)
                    kv_wr[("v", blk, chf)] = tok_e
                    pslast["a%d" % slot] = tok_e
            if tt == 0 and half == 0:
                ddump("dbg_q", q_t[:], [q_wr.get(i) for i in range(8)])
                ddump("dbg_k", kring[:], [kv_wr.get(("k", b, kc)) for b in range(4) for kc in range(8)])
                ddump("dbg_v", vring[:], [kv_wr.get(("v", b, cf)) for b in range(4) for cf in range(2)])
            xw_rd[0] = last_xn_reader
            _cut("qkv%d_%d" % (tt, half))

            pcs = pieces_of(Q)
            jmin = pcs[0][0]
            ecum = {}
            _c = 0
            for (j, plo, phi) in pcs:
                ecum[j] = _c
                _c += phi - plo
            o_toks = []
            last_score_mm = None
            for grp in range(NH // HG):
                for (j, plo, phi) in pcs:
                    w = phi - plo
                    moff = plo - 128 * j
                    blk = ringblk(j)
                    for hi_ in range(HG):
                        h = grp * HG + hi_
                        ch = h // 2
                        pb = 64 * (h % 2)
                        psl = ps_a[hi_]
                        eslot = (j - jmin) * HG + hi_
                        ecol = hi_ * 1536 + ecum[j]
                        dep_mm = [kv_wr.get(("k", blk, ch)), q_wr.get(ch),
                                  pslast.get("a%d" % hi_), ering_rd.get(eslot)]
                        tok_mm = s.pe((lambda e, ch=ch, pb=pb, blk=blk, plo=plo, w=w, psl=psl, Q=Q:
                                       e.matmul(psl[:, 0:w],
                                                kring[pb:pb + 64, (ch * NRING + blk) * 128:(ch * NRING + blk) * 128 + 128],
                                                q_t[pb:pb + 64, ch * 512 + (plo - Q): ch * 512 + (plo - Q) + w],
                                                start=True, stop=True)), deps=dep_mm, ms=True)
                        last_score_mm = tok_mm
                        tok_ex = s.act((lambda e, ecol=ecol, psl=psl, w=w: e.activation(
                            ering[:, ecol:ecol + w], psl[:, 0:w], AF.Exp, scale=0.125)),
                            deps=[tok_mm], ms=True)
                        pslast["a%d" % hi_] = tok_ex
                        tok_k = s.pool((lambda e, ecol=ecol, moff=moff, w=w: e.tensor_mul(
                            ering[:, ecol:ecol + w], ering[:, ecol:ecol + w],
                            mask[:, moff:moff + w])), deps=[tok_ex], ms=True)
                        kv_wr[("e", eslot)] = tok_k
                for hi_ in range(HG):
                    h = grp * HG + hi_
                    psl = ps_b[hi_]
                    deps0 = [pslast.get("b%d" % hi_), pslast.get("b%dr" % hi_)]
                    for (j, plo, phi) in pcs:
                        deps0 += [kv_wr.get(("e", (j - jmin) * HG + hi_)),
                                  kv_wr.get(("v", ringblk(j), 0)),
                                  kv_wr.get(("v", ringblk(j), 1))]
                    first = True
                    for (j, plo, phi) in pcs:
                        blk = ringblk(j)
                        ecol = hi_ * 1536 + ecum[j]
                        s.pe((lambda e, blk=blk, h=h, plo=plo, phi=phi, ecol=ecol, psl=psl, Q=Q, first=first:
                              e.matmul(psl[0:65, plo - Q:phi - Q],
                                       vring[:, blk * VS + h * 65: blk * VS + (h + 1) * 65],
                                       ering[:, ecol:ecol + (phi - plo)],
                                       start=first, stop=False)), deps=deps0)
                        first = False
                    tok_mm = s.milestone("tensor")
                    for (j, plo, phi) in pcs:
                        ering_rd[(j - jmin) * HG + hi_] = tok_mm
                        kv_rd[("v", ringblk(j))] = tok_mm
                        kv_rd[ringblk(j)] = tok_mm
                    ob = osb[:, 0:512]
                    tok_c = s.dve((lambda e, psl=psl, ob=ob: e.tensor_copy(ob, psl[0:65, :])),
                                  deps=[tok_mm], ms=True)
                    rc = rcr[0:1, 2048:2560]
                    rct = rows[0:1, 1024:1536]
                    tok_r = s.act((lambda e, ob=ob, rct=rct: e.activation(
                        rct, ob[64:65, :], AF.Ln)), deps=[tok_c, rc_rd[0]], ms=True)
                    tok_r = s.act((lambda e, rc=rc, rct=rct: e.activation(
                        rc, rct, AF.Exp, scale=-1.0)), deps=[tok_r], ms=True)
                    pslast["b%d" % hi_] = tok_c
                    pslast["b%dr" % hi_] = tok_r
                    tok_b = s.pe((lambda e, rc=rc: e.matmul(
                        ps_c[0:64, :], ones_rf[0:1, 0:64], rc, start=True, stop=True)),
                        deps=[tok_r, pslast.get("c")], ms=True)
                    rc_rd[0] = tok_b
                    bcs = bcsb[0:64, 0:512]
                    tok_bc = s.dve((lambda e, bcs=bcs: e.tensor_copy(bcs, ps_c[0:64, :])),
                                   deps=[tok_b], ms=True)
                    pslast["c"] = tok_bc
                    ch = h // 2
                    pb = 64 * (h % 2)
                    tok_o = s.dve((lambda e, ch=ch, pb=pb, ob=ob, bcs=bcs: e.tensor_mul(
                        xw_t[pb:pb + 64, ch * 512:(ch + 1) * 512], ob[0:64, :], bcs)),
                        deps=[tok_bc, xw_rd[0]], ms=True)
                    o_toks.append(tok_o)
            if tt == 0 and half == 0:
                ddump("dbg_o", xw_t[:], o_toks)
                ddump("dbg_e", ering[:], o_toks)
            q_rd[0] = last_score_mm
            _cut("av%d_%d" % (tt, half))

            pj_toks = []
            last_o_reader = None
            for cc in range(NCH):
                slot = cc % 2
                wdst = wdb[:, slot * 1024:(slot + 1) * 1024]
                tok_w = s.dma("gpsimd", (lambda e, cc=cc, wdst=wdst: e.dma_start(
                    out=wdst.rearrange("p (f o) -> p f o", o=128),
                    in_=w_proj[:, cc * 128:(cc + 1) * 128].rearrange(
                        "(f p) o -> p f o", p=128))),
                    deps=[buf_rd.get("wdb%d" % slot)], sem="dwa%d" % slot)
                dep_mm = [tok_w, pslast.get("a%d" % slot)] + o_toks
                for fc in range(NCH):
                    s.pe((lambda e, fc=fc, slot=slot, wdst=wdst: e.matmul(
                        ps_a[slot][:], wdst[:, fc * 128:(fc + 1) * 128],
                        xw_t[:, fc * 512:(fc + 1) * 512],
                        start=(fc == 0), stop=(fc == NCH - 1))), deps=dep_mm)
                tok_mm = s.milestone("tensor")
                buf_rd["wdb%d" % slot] = tok_mm
                last_o_reader = tok_mm
                tok_e = s.act((lambda e, cc=cc, slot=slot: e.activation(
                    y2sb[:, cc * 512:(cc + 1) * 512], ps_a[slot][:],
                    AF.Identity, bias=bproj_c[:, cc:cc + 1])),
                    deps=[tok_mm, y2_rd[0], q_rd[0]], ms=True)
                pslast["a%d" % slot] = tok_e
                pj_toks.append(tok_e)
            if tt == 0 and half == 0:
                ddump("dbg_pj", y2sb[:], pj_toks)
            xw_rd[0] = last_o_reader

            row_reset(0)
            pm_row, pr_row, tok_lnp = ln_stats(
                lambda c_: y2sb[:, c_ * 512:(c_ + 1) * 512], 512, pj_toks)
            tok_mb3 = bcast_row(pm_row, m3bc[:], 512, [tok_lnp])
            tok_rb3 = bcast_row(pr_row, r3bc[:], 512, [tok_lnp])
            ks3 = k_sub % 3
            x1_toks = []
            for c_ in range(NCH):
                x1dst = x1ring[:, (ks3 * NCH + c_) * 512:(ks3 * NCH + c_ + 1) * 512]
                dep = [tok_mb3, tok_rb3, pj_toks[c_], x1_rd.get(k_sub - 3)]
                s.dve((lambda e, c_=c_, x1dst=x1dst: e.tensor_sub(
                    x1dst, y2sb[:, c_ * 512:(c_ + 1) * 512], m3bc[:])), deps=dep)
                s.dve((lambda e, x1dst=x1dst: e.tensor_mul(x1dst, x1dst, r3bc[:])))
                tok = s.dve((lambda e, c_=c_, x1dst=x1dst: e.tensor_add(
                    x1dst, x1dst, xb_ch(c_))), deps=[tok_x], ms=True)
                x1_toks.append(tok)
            if tt == 0 and half == 0:
                ddump("dbg_x1", x1ring[:, 0:NCH * SUB], x1_toks)
            x1_wr[k_sub] = x1_toks
            y2_rd[0] = x1_toks[-1]
            prev_xload[0] = x1_toks[-1]
            _cut("x1_%d_%d" % (tt, half))
        _cut("attn%d" % tt)
        kA, kB = 2 * tt, 2 * tt + 1
        xqA, hsA = quant(kA)
        if tt == 0:
            ddump("dbg_xq", xqt[:], xqA)
        _cut("quantA%d" % tt)
        toksA = w1_mm(kA, xqA, hsA)
        if tt == 0:
            ddump("dbg_h", h2[:, 0:HCH * 514], toksA)
        _cut("w1A%d" % tt)
        if not seq_first:
            conv_block(kA - 1, seq_last_sub=False)
        xqB, hsB = quant(kB)
        w1_mm(kB, xqB, hsB)
        conv_block(kA, seq_last_sub=False)
        if seq_last:
            conv_block(kB, seq_last_sub=True)
        _cut("tile%d" % tt)

    try:
        _tiles()
    except _StopBuild:
        pass

    return _finish(nc, ctxs, s)


def _finish(nc, ctxs, s):
    final_deps = [(nm, s.count[nm]) for nm in Sched.DMA_SEMS if nm in s.count]
    final_deps += [("pe", s.count.get("pe", 0)), ("dve", s.count.get("dve", 0)),
                   ("act", s.count.get("act", 0))]
    s._op("gpsimd", lambda e: e.nop(), final_deps)

    sem_names = ["pe", "dve", "act", "pool"] + Sched.DMA_SEMS
    sem_ctxs = {}
    sems = {}
    for n in sem_names:
        g = nc.semaphore(n)
        sems[n] = g.__enter__()
        sem_ctxs[n] = g
    s.emit_all(nc, sems)
    for n in reversed(sem_names):
        sem_ctxs[n].__exit__(None, None, None)
    for g in reversed(ctxs):
        g.__exit__(None, None, None)
    return nc


_NC_CACHE = [None]


def _host_constants():
    i = np.arange(128)[:, None]
    jj = np.arange(EW)[None, :]
    band = ((jj >= i) & (jj <= i + WIN)).astype(ml_dtypes.bfloat16)
    ident = np.eye(128, dtype=np.float32).astype(ml_dtypes.bfloat16)
    return band, ident


def kernel(L0_attn_x, L0_attn_W_attn, L0_attn_b_attn, L0_attn_W_proj, L0_attn_b_proj,
           L0_ffn_w1, L0_ffn_conv_w, L0_ffn_conv_b, L0_ffn_w2):
    x = np.ascontiguousarray(L0_attn_x, dtype=np.float32)
    band, ident = _host_constants()
    ba = np.asarray(L0_attn_b_attn, np.float32)
    bp = np.asarray(L0_attn_b_proj, np.float32)
    cw = np.asarray(L0_ffn_conv_w, np.float32).reshape(HID, 3)
    cb = np.asarray(L0_ffn_conv_b, np.float32)
    base = {
        "w_attn": np.ascontiguousarray(L0_attn_W_attn, np.float32),
        "b_attn": np.ascontiguousarray(ba),
        "w_proj": np.ascontiguousarray(L0_attn_W_proj, np.float32),
        "b_proj": np.ascontiguousarray(bp),
        "w1": np.ascontiguousarray(L0_ffn_w1, np.float32),
        "convw": np.ascontiguousarray(cw),
        "convb": np.ascontiguousarray(cb),
        "w2": np.ascontiguousarray(L0_ffn_w2, np.float32),
        "maskc": band,
        "idc": ident,
        "battn_pc": np.ascontiguousarray(ba.reshape(3 * NCH, 128).T),
        "bproj_pc": np.ascontiguousarray(bp.reshape(NCH, 128).T),
        "convw_pc": np.ascontiguousarray(cw.reshape(HCH, 128, 3).transpose(1, 0, 2).reshape(128, 3 * HCH)),
        "convb_pc": np.ascontiguousarray(cb.reshape(HCH, 128).T),
    }
    in_maps = []
    for core in range(8):
        shard = x[core * SEQ:(core + 1) * SEQ]
        xt = np.ascontiguousarray(np.transpose(shard, (0, 2, 1)))
        m = dict(base)
        m["x_in"] = xt
        in_maps.append(m)

    if _NC_CACHE[0] is None:
        _NC_CACHE[0] = build_program()
    nc = _NC_CACHE[0]
    res = run_bass_kernel_spmd(nc, in_maps, core_ids=list(range(8)))
    outs = []
    for core in range(8):
        ot = res.results[core]["out_t"]
        outs.append(np.transpose(ot, (0, 2, 1)))
    return np.ascontiguousarray(np.concatenate(outs, axis=0), np.float32)


if __name__ == "__main__":
    build_program()
    print("build ok")
